# version 30
# baseline (speedup 1.0000x reference)
import contextlib
import dataclasses
import math

import numpy as np
import ml_dtypes

import concourse.bass as bass
import concourse.mybir as mybir
from concourse.bass_utils import run_bass_kernel_spmd

F32 = mybir.dt.float32
BF16 = mybir.dt.bfloat16
I32 = mybir.dt.int32
AF = mybir.ActivationFunctionType
ALU = mybir.AluOpType
AX = mybir.AxisListType.X

NCORES = 8
S = 4096
D = 1024
DH = 2048
NQ = 8
NMH = 16
NE = 8
FF = 1408
NJ = 11
TG = 1024
NG = 8
W = 128
PCAP = 768
PCT = PCAP // 128
CW = (512, 256)
NW = TG // W
NG_RUN = NG
MLQ = "s"
YQ = "g"
IL_UPTO = NG
DEBUG = False
SKIP_MOE = False
INV4PI = 1.0 / (4.0 * math.pi)
TWO_PI = 2.0 * math.pi


STREAMS = ("t", "v", "a", "s", "g")


DMA_LANES = {"s": 16, "g": 8, "o": 6, "c": 6}
DMA_ENG = {"s": "s", "g": "g", "o": "g", "c": "g"}


class Em:

    def __init__(self, phase, marks=None):
        self.phase = phase
        self.active = None
        self.engines = {}
        self.sems = {}
        self.cnt = {k: 0 for k in ("t", "v", "a")}
        self.dma_idx = {k: 0 for k in DMA_LANES}
        self.lane_cnt = {k: [0] * n for k, n in DMA_LANES.items()}
        self.last_dma = {k: None for k in DMA_LANES}
        self.marks = {} if phase == 1 else marks
        self.last = {k: None for k in ("t", "v", "a")}
        self.seen = {k: {} for k in STREAMS}

    def op(self, st, fn):
        if self.phase == 2 and self.active == st:
            self.last[st] = fn(self.engines[st])

    def dma(self, st, fn):
        eng_st = DMA_ENG[st]
        i = self.dma_idx[st]
        self.dma_idx[st] = i + 1
        lane = i % DMA_LANES[st]
        key = st + str(lane)
        prev = self.lane_cnt[st][lane] * 16
        self.lane_cnt[st][lane] += 1
        self.last_dma[st] = (key, prev + 16)
        if self.phase == 2 and self.active == eng_st:
            eng = self.engines[eng_st]
            if prev > 0 and self.seen[eng_st].get(key, -1) < prev:
                eng.wait_ge(self.sems[key], prev)
                self.seen[eng_st][key] = prev
            inst = fn(eng)
            inst.then_inc(self.sems[key], 16)

    def dmark(self, st, label):
        if self.phase == 1:
            assert label not in self.marks, label
            self.marks[label] = self.last_dma[st]

    def dbar(self, dma_st, label):
        if self.phase == 1:
            assert label not in self.marks, label
            pairs = [(dma_st + str(ln), c * 16)
                     for ln, c in enumerate(self.lane_cnt[dma_st]) if c > 0]
            self.marks[label] = ("MULTI", pairs)

    def mark(self, st, label):
        self.cnt[st] += 1
        if self.phase == 1:
            assert label not in self.marks, label
            self.marks[label] = (st, self.cnt[st])
        elif self.active == st:
            li = self.last[st]
            if li is None:
                li = self.engines[st].nop()
            li.then_inc(self.sems[st], 1)
            self.last[st] = None

    def alias(self, label, src):
        if self.phase == 1:
            assert label not in self.marks, label
            self.marks[label] = self.marks[src]

    def wait(self, st, label):
        if self.phase == 1:
            return
        prod, val = self.marks[label]
        if self.active != st:
            return
        pairs = val if prod == "MULTI" else [(prod, val)]
        for p_, v_ in pairs:
            if self.seen[st].get(p_, -1) >= v_:
                continue
            self.engines[st].wait_ge(self.sems[p_], v_)
            self.seen[st][p_] = v_

    def opsync(self, st, fn):
        self.op(st, fn)
        self._ss = getattr(self, "_ss", 0) + 1
        lbl = ("ss", self._ss)
        self.mark(st, lbl)
        self.wait(st, lbl)

    def drain_dma(self, st, dma_st):
        if self.phase == 1:
            return
        if self.active != st:
            return
        for lane in range(DMA_LANES[dma_st]):
            key = dma_st + str(lane)
            val = self.lane_cnt[dma_st][lane] * 16
            if val > 0 and self.seen[st].get(key, -1) < val:
                self.engines[st].wait_ge(self.sems[key], val)
                self.seen[st][key] = val



def _vec_tile(v, n):
    return np.ascontiguousarray(v.reshape(n, 128).T)


def _prep(kw):
    x = np.asarray(kw["L0_attn_x"], dtype=np.float32)
    w_in = np.asarray(kw["L0_attn_w_in"], np.float32)
    b_in = np.asarray(kw["L0_attn_b_in"], np.float32)
    w_gx = np.asarray(kw["L0_attn_w_gx"], np.float32)
    b_gx = np.asarray(kw["L0_attn_b_gx"], np.float32)
    w_ga = np.asarray(kw["L0_attn_w_ga"], np.float32)
    b_ga = np.asarray(kw["L0_attn_b_ga"], np.float32)
    arp = np.asarray(kw["L0_attn_a_real_param"], np.float32)
    aip = np.asarray(kw["L0_attn_a_imag_param"], np.float32)
    w_out = np.asarray(kw["L0_attn_w_out"], np.float32)
    b_out = np.asarray(kw["L0_attn_b_out"], np.float32)
    rw = np.asarray(kw["L0_ffn_router_weight"], np.float32)
    w1 = np.asarray(kw["L0_ffn_w1"], np.float32)
    w2 = np.asarray(kw["L0_ffn_w2"], np.float32)

    com = {}
    com["w_in_t"] = np.ascontiguousarray(
        w_in.reshape(8, 128, 16, 128).transpose(2, 1, 0, 3))
    com["w_out_t"] = np.ascontiguousarray(
        w_out.reshape(16, 128, 8, 128).transpose(2, 1, 0, 3))
    com["w_gx_t"] = np.ascontiguousarray(
        w_gx.reshape(8, 2, 128, 128).transpose(2, 0, 1, 3))
    com["w_ga_t"] = np.ascontiguousarray(
        w_ga.reshape(8, 2, 128, 128).transpose(2, 0, 1, 3))
    com["w1_t"] = np.ascontiguousarray(
        w1.reshape(8, 8, 128, 22, 128).transpose(0, 3, 2, 1, 4))
    com["w2_t"] = np.ascontiguousarray(w2.reshape(8, 11, 128, 1024))
    com["rw_t"] = np.ascontiguousarray(rw.reshape(8, 128, 8).transpose(1, 0, 2))
    com["cin"] = np.ascontiguousarray(np.concatenate(
        [_vec_tile(arp, 8), _vec_tile(aip, 8),
         _vec_tile(b_gx, 8), _vec_tile(b_ga, 8)], axis=1))
    com["binv"] = _vec_tile(b_in, 16)
    com["boutv"] = _vec_tile(b_out, 8)
    com["identf"] = np.eye(128, dtype=np.float32)
    com["identb"] = np.eye(128, dtype=ml_dtypes.bfloat16)
    com["iota"] = np.arange(128, dtype=np.float32).reshape(128, 1)
    com["bigc"] = np.full((NE, PCAP, 2), 1e6, dtype=np.float32)
    return x, com



def _emit(E, T):
    x_d = T["x"]
    out_d = T["out"]

    pl = T["pl"]
    PS = T["ps"]
    cvec, cin, binv, lnt, carry = T["cvec"], T["cin"], T["binv"], T["lnt"], T["carry"]
    xng, gmoe = T["xng"], T["gmoe"]
    wgates, rw_sb, win_sb, wout_sb = T["wgates"], T["rw_sb"], T["win_sb"], T["wout_sb"]
    y1t, y2t, silu_sb = T["y1t"], T["y2t"], T["silu_sb"]
    identf, identb = T["identf"], T["identb"]
    rlg, ltok, lp, msk, rtm, mfm, posf, rcount, idxf, idxi, tokv, iota_sb = (
        T["rlg"], T["ltok"], T["lp"], T["msk"], T["rtm"], T["mfm"],
        T["posf"], T["rcount"], T["idxf"], T["idxi"], T["tokv"], T["iota_sb"])
    xef, xetm, w2rt, o_tok, pslot, sidxi, yfm, otm = (
        T["xef"], T["xetm"], T["w2rt"], T["o_tok"], T["pslot"],
        T["sidxi"], T["yfm"], T["otm"])

    for nm, dst, src in [("c_identf", identf, T["identf_d"]),
                         ("c_identb", identb, T["identb_d"]),
                         ("c_iota", iota_sb, T["iota_d"]),
                         ("c_cin", cin, T["cin_d"]),
                         ("c_binv", binv, T["binv_d"])]:
        E.dma("s", lambda s, d=dst, sr=src: s.dma_start(out=d[:], in_=sr))
        E.dmark("s", nm)
    E.dma("s", lambda s: s.dma_start(out=cvec[:, 40:48], in_=T["boutv_d"]))
    E.dmark("s", "c_bout")

    E.dma("g", lambda g: g.dma_start(
        out=wgates[:, 0:2048],
        in_=T["w_gx_d"].rearrange("p n k c -> p (n k c)")))
    E.dmark("g", "c_wgx")
    E.dma("g", lambda g: g.dma_start(
        out=wgates[:, 2048:4096],
        in_=T["w_ga_d"].rearrange("p n k c -> p (n k c)")))
    E.dmark("g", "c_wga")
    E.dma("g", lambda g: g.dma_start(
        out=rw_sb[:], in_=T["rw_d"].rearrange("p k j -> p (k j)")))
    E.dmark("g", "c_rw")
    E.dma("g", lambda g: g.dma_start(out=T["wibf"].ap(), in_=T["w_in_d"]))
    E.dmark("g", "wibf")
    E.dma("g", lambda g: g.dma_start(out=T["wobf"].ap(), in_=T["w_out_d"]))
    E.dmark("g", "wobf")
    for e in range(NE):
        E.dma("g", lambda g, e=e: g.dma_start(
            out=T["w1bf"].ap()[e], in_=T["w1_d"][e]))
        E.dmark("g", ("cast_w1", e))
        E.dma("g", lambda g, e=e: g.dma_start(
            out=T["w2bf"].ap()[e], in_=T["w2_d"][e]))
        E.dmark("g", ("cast_w2", e))

    E.wait("a", "c_cin")
    E.opsync("a", lambda a: a.activation(out=cvec[:, 48:56], in_=cin[:, 0:8],
                                         func=AF.Abs))
    E.opsync("a", lambda a: a.activation(out=cvec[:, 56:64], in_=cvec[:, 48:56],
                                         func=AF.Exp, scale=-1.0))
    E.op("a", lambda a: a.activation(out=cvec[:, 48:56], in_=cvec[:, 56:64],
                                     func=AF.Ln, bias=1.0))
    E.mark("a", "sp_ln")
    E.wait("v", "c_cin")
    E.wait("v", "c_bout")
    E.wait("v", "sp_ln")
    E.opsync("v", lambda v: v.tensor_scalar(out=cvec[:, 56:64], in0=cin[:, 0:8],
                                            scalar1=0.0, scalar2=None, op0=ALU.max))
    E.opsync("v", lambda v: v.tensor_tensor(out=cvec[:, 48:56], in0=cvec[:, 48:56],
                                            in1=cvec[:, 56:64], op=ALU.add))
    E.opsync("v", lambda v: v.tensor_scalar(out=cvec[:, 0:8], in0=cvec[:, 48:56],
                                            scalar1=-4.0, scalar2=None, op0=ALU.mult))
    E.opsync("v", lambda v: v.tensor_scalar(out=cvec[:, 8:16], in0=cvec[:, 48:56],
                                            scalar1=-8.0, scalar2=None, op0=ALU.mult))
    E.opsync("v", lambda v: v.tensor_scalar(out=cvec[:, 16:24], in0=cin[:, 8:16],
                                            scalar1=INV4PI, scalar2=None, op0=ALU.mult))
    E.opsync("v", lambda v: v.tensor_scalar(out=cvec[:, 24:32], in0=cin[:, 16:24],
                                            scalar1=0.5, scalar2=None, op0=ALU.mult))
    E.opsync("v", lambda v: v.tensor_scalar(out=cvec[:, 32:40], in0=cin[:, 24:32],
                                            scalar1=0.5, scalar2=None, op0=ALU.mult))
    E.mark("v", "cvec")

    REG = {}

    def _mkregs(gg):
        r1 = gg.alloc_register("cap_bound")
        gg.reg_mov(r1, PCAP - 1)
        r2 = gg.alloc_register("tok_bound")
        gg.reg_mov(r2, S - 1)
        REG["cap"] = r1
        REG["tok"] = r2
        return gg.engine_nop()

    if not SKIP_MOE:
        E.op("g", _mkregs)

    bank_use = {k: 0 for k in PS}
    bank_rel = {k: 0 for k in PS}

    def bank_acquire(bank, st):
        n = bank_use[bank]
        bank_use[bank] = n + 1
        if n > 0:
            E.wait(st, ("bk", bank, n - 1))
        return n

    def bank_release(bank, src_label):
        n = bank_rel[bank]
        bank_rel[bank] = n + 1
        E.alias(("bk", bank, n), src_label)

    def widx_of(g, i):
        return g * NW + i

    def sync_slotA(g, i):
        if (i == -1 or i % 2 == 1) and 0 <= i + 1 < NW:
            pb = widx_of(g, i + 1) // 2
            for u2 in range(2 * NMH):
                m, h = u2 // 2, u2 % 2
                u = pb * 2 * NMH + u2
                slot = u % 6
                if u >= 6:
                    pu = u - 6
                    E.wait("s", ("xh_ps", pu // (2 * NMH),
                                 (pu % (2 * NMH)) // 2))
                else:
                    E.wait("s", "wibf")
                E.dma("s", lambda s, m=m, h=h, slot=slot: s.dma_start(
                    out=win_sb[:, slot * 512:(slot + 1) * 512],
                    in_=T["wibf"].ap()[m, :, 4 * h:4 * (h + 1), :].rearrange(
                        "p k c -> p (k c)")))
                E.dmark("s", ("winL", u))
    def sync_woL(g, i):
        if 0 <= i < NW:
            wx = widx_of(g, i)
            for u2 in range(2 * NQ):
                m, h = u2 // 2, u2 % 2
                u = wx * 2 * NQ + u2
                slot = u % 4
                if u >= 4:
                    pu = u - 4
                    E.wait("s", ("wo_ps", pu // (2 * NQ), pu % (2 * NQ)))
                else:
                    E.wait("s", "wobf")
                E.dma("s", lambda s, m=m, h=h, slot=slot: s.dma_start(
                    out=wout_sb[:, slot * 1024:(slot + 1) * 1024],
                    in_=T["wobf"].ap()[m, :, 8 * h:8 * (h + 1), :].rearrange(
                        "p k c -> p (k c)")))
                E.dmark("s", ("woL", u))

    def sync_loadx(g, i):
        wx = widx_of(g, i)
        par = wx % 3
        seq = g // 4
        t0 = (g % 4) * TG + i * W
        if wx >= 3:
            E.wait("s", ("osum", wx - 3))
        E.dma("s", lambda s: s.dma_start(
            out=pl["xtm"][par][:], in_=x_d[seq, t0:t0 + W, :]))
        E.dmark("s", ("xL", wx))

    def pe_router(g, i):
        wx = widx_of(g, i)
        if SKIP_MOE:
            return
        E.wait("t", "c_rw")
        bank_acquire("pR", "t")
        for k in range(NQ):
            E.op("t", lambda t, k=k: t.matmul(
                out=PS["pR"][0:8, 0:W],
                lhsT=rw_sb[:, k * 8:(k + 1) * 8],
                rhs=xng[:, (k * 2 + i % 2) * W:
                        (k * 2 + i % 2 + 1) * W],
                start=(k == 0), stop=(k == NQ - 1)))
        E.mark("t", ("lg_ps", wx))
        E.wait("t", "c_identf")
        E.wait("t", ("lg_ev", wx))
        bank_acquire("pR", "t")
        E.op("t", lambda t: t.transpose(
            out=PS["pR"][:, 0:8], in_=rlg[0:8, 0:W],
            identity=identf[0:8, 0:8]))
        E.mark("t", ("ltr", wx))

    def pe_slotA(g, i):
        wx = widx_of(g, i)
        if 0 <= i < NW and i % 2 == 0:
            E.wait("t", "c_identb")
            for w2 in range(2):
                ww = wx + w2
                E.wait("t", ("xn_tm", ww))
                bank_acquire("pT", "t")
                for q in range(NQ):
                    E.op("t", lambda t, q=q, pp=ww % 2: t.transpose(
                        out=PS["pT"][:, q * 128:(q + 1) * 128],
                        in_=pl["xntm"][pp][:, q * 128:(q + 1) * 128],
                        identity=identb[:]))
                E.mark("t", ("xnT", ww))
            E.wait("t", ("xn_tev", wx))
            pe_router(g, i)
            E.wait("t", ("xn_tev", wx + 1))
            pb = wx // 2
            for m in range(NMH):
                u0 = (pb * NMH + m) * 2
                bank = "pA" if m % 2 == 0 else "pB"
                E.wait("t", ("winL", u0))
                bank_acquire(bank, "t")
                for w2 in range(2):
                    for k in range(NQ):
                        if w2 == 0 and k == 4:
                            E.wait("t", ("winL", u0 + 1))
                        hslot = (u0 + k // 4) % 6
                        E.op("t", lambda t, k=k, hslot=hslot, w2=w2, bank=bank:
                             t.matmul(
                                 out=PS[bank][:, w2 * W:(w2 + 1) * W],
                                 lhsT=win_sb[:, (hslot * 4 + k % 4) * 128:
                                             (hslot * 4 + k % 4 + 1) * 128],
                                 rhs=xng[:, (k * 2 + (i + w2) % 2) * W:
                                         (k * 2 + (i + w2) % 2 + 1) * W],
                                 start=(k == 0), stop=(k == NQ - 1)))
                E.mark("t", ("xh_ps", pb, m))
        if 0 <= i < NW:
            par = wx % 2
            E.wait("t", "c_wgx")
            E.wait("t", "c_wga")
            for gate in range(2):
                for n_ in range(NQ):
                    bank = "pC" if n_ % 2 == 0 else "pD"
                    E.wait("t", ("xh_ev", wx, 2 * n_ + 1))
                    bank_acquire(bank, "t")
                    for k in range(2):
                        tile = gate * 16 + n_ * 2 + k
                        E.op("t", lambda t, tile=tile, n_=n_, gate=gate, k=k: t.matmul(
                            out=PS[("pC" if n_ % 2 == 0 else "pD")][:, 0:W],
                            lhsT=wgates[:, tile * 128:(tile + 1) * 128],
                            rhs=pl["xh"][wx % 3][:, (2 * n_ + k) * 128:(2 * n_ + k + 1) * 128],
                            start=(k == 0), stop=(k == 1)))
                    E.mark("t", ("g_ps", wx, gate, n_))
            if i % 2 == 1:
                pe_router(g, i)
        if 1 <= i <= NW:
            pwx = widx_of(g, i - 1)
            E.wait("t", ("hm", pwx))
            for m in range(NQ):
                bank = "pE" if m % 2 == 0 else "pF"
                bank_acquire(bank, "t")
                for h in range(2):
                    u = pwx * 2 * NQ + m * 2 + h
                    E.wait("t", ("woL", u))
                    slot = u % 4
                    for k in range(8 * h, 8 * h + 8):
                        E.op("t", lambda t, k=k, h=h, slot=slot, bank=bank:
                             t.matmul(
                                 out=PS[bank][:, 0:W],
                                 lhsT=wout_sb[:, (slot * 8 + k - 8 * h) * 128:
                                              (slot * 8 + k - 8 * h + 1) * 128],
                                 rhs=pl["hm"][:, k * 128:(k + 1) * 128],
                                 start=(k == 0), stop=(k == NMH - 1)))
                    E.mark("t", ("wo_ps", pwx, m * 2 + h))
        if 0 <= i < NW and not SKIP_MOE:
            E.wait("t", ("msk_rdy", wx))
            bank_acquire("pR", "t")
            E.op("t", lambda t: t.transpose(
                out=PS["pR"][0:8, 0:128], in_=msk[:, 0:8],
                identity=identf[:]))
            E.mark("t", ("mtr", wx))
            E.wait("t", ("pos_rdy", wx))
            bank_acquire("pR", "t")
            E.op("t", lambda t: t.transpose(
                out=PS["pR"][:, 0:8], in_=posf[0:8, 0:W],
                identity=identf[0:8, 0:8]))
            E.mark("t", ("ptr2", wx))
        if 1 <= i <= NW:
            pwx = widx_of(g, i - 1)
            E.wait("t", ("yfm", pwx))
            for half in range(2):
                ob = "pE" if half == 0 else "pF"
                bank_acquire(ob, "t")
                for q in range(4):
                    qq = half * 4 + q
                    E.op("t", lambda t, q=q, qq=qq, ob=ob: t.transpose(
                        out=PS[ob][:, q * 128:(q + 1) * 128],
                        in_=yfm[:, qq * 128:(qq + 1) * 128],
                        identity=identf[:]))
                E.mark("t", ("ot_ps", pwx, half))
    def act_slotA(g, i):
        wx = widx_of(g, i)
        if 1 <= i <= NW:
            pwx = widx_of(g, i - 1)
            if pwx >= 1:
                E.wait("a", ("hm", pwx - 1))
            E.wait("a", ("fr", pwx))
            E.op("a", lambda a: a.activation(out=pl["sn"][:], in_=pl["p1"][:],
                                             func=AF.Sin, scale=TWO_PI))
            E.mark("a", ("sn", pwx))
            E.wait("a", ("fr2", pwx))
            E.op("a", lambda a: a.activation(out=pl["cs"][:], in_=pl["p2"][:],
                                             func=AF.Sin, scale=TWO_PI))
            E.mark("a", ("cs", pwx))
        if 0 <= i + 1 < NW:
            nwx = widx_of(g, i + 1)
            p1_ = nwx % 2
            E.wait("a", ("var", nwx))
            E.opsync("a", lambda a, p1_=p1_: a.activation(
                out=lnt[:, 10:11], in_=lnt[:, 6 + 3 * p1_:7 + 3 * p1_],
                func=AF.Ln))
            E.op("a", lambda a, p1_=p1_: a.activation(
                out=lnt[:, 2 * p1_ + 1:2 * p1_ + 2], in_=lnt[:, 10:11],
                func=AF.Exp, scale=-0.5))
            E.mark("a", ("rstd", nwx))
        if 0 <= i < NW and i % 2 == 0:
            for w2 in range(2):
                ww = wx + w2
                E.wait("a", ("xnT", ww))
                E.op("a", lambda a, iw=(i + w2) % 2: a.copy(
                    out=xng[:].rearrange("p (q t) -> p q t", q=NQ)[
                        :, :, iw * W:(iw + 1) * W],
                    in_=PS["pT"][:, 0:1024]))
                E.mark("a", ("xn_tev", ww))
                bank_release("pT", ("xn_tev", ww))
            if not SKIP_MOE:
                E.wait("a", ("lg_ps", wx))
                E.op("a", lambda a: a.copy(out=rlg[0:8, 0:W],
                                           in_=PS["pR"][0:8, 0:W]))
                E.mark("a", ("lg_ev", wx))
                bank_release("pR", ("lg_ev", wx))
            E.wait("a", "cvec")
            E.wait("a", "c_binv")
            pb = wx // 2
            for m in range(NMH):
                bank = "pA" if m % 2 == 0 else "pB"
                E.wait("a", ("xh_ps", pb, m))
                for w2 in range(2):
                    ww = wx + w2
                    if m == 0 and ww >= 3:
                        E.wait("a", ("hm", ww - 3))
                    E.op("a", lambda a, m=m, w2=w2, bank=bank, r3=(wx + w2) % 3:
                         a.activation(
                             out=pl["xh"][r3][:, m * 128:(m + 1) * 128],
                             in_=PS[bank][:, w2 * W:(w2 + 1) * W],
                             func=AF.Identity, bias=binv[:, m:m + 1]))
                    E.mark("a", ("xh_ev", ww, m))
                bank_release(bank, ("xh_ev", wx + 1, m))
        if 0 <= i < NW:
            par = wx % 2
            for gate in range(2):
                dst = pl["tx"][par] if gate == 0 else pl["ta"]
                bias0 = 24 if gate == 0 else 32
                for n_ in range(NQ):
                    E.wait("a", ("g_ps", wx, gate, n_))
                    E.op("a", lambda a, dst=dst, n_=n_, bias0=bias0: a.activation(
                        out=dst[:, n_ * 128:(n_ + 1) * 128],
                        in_=PS[("pC" if n_ % 2 == 0 else "pD")][:, 0:W],
                        func=AF.Tanh, scale=0.5, bias=cvec[:, bias0 + n_:bias0 + n_ + 1]))
                    E.mark("a", ("g_ev", wx, gate, n_))
                    bank_release("pC" if n_ % 2 == 0 else "pD", ("g_ev", wx, gate, n_))
            if not SKIP_MOE and i % 2 == 1:
                E.wait("a", ("lg_ps", wx))
                E.op("a", lambda a: a.copy(out=rlg[0:8, 0:W], in_=PS["pR"][0:8, 0:W]))
                E.mark("a", ("lg_ev", wx))
                bank_release("pR", ("lg_ev", wx))
        if 0 <= i < NW and not SKIP_MOE:
            rpar = wx % 2
            E.wait("a", ("rsub", wx))
            E.op("a", lambda a, rpar=rpar: a.activation(
                out=lp[rpar][:], in_=ltok[:], func=AF.Exp))
            E.mark("a", ("rexp", wx))
        if 0 <= i < NW:
            par = wx % 2
            for q in range(NQ):
                E.op("a", lambda a, q=q, par=par: a.activation(
                    out=pl["rho"][par][:, q * 128:(q + 1) * 128],
                    in_=pl["ta"][:, q * 128:(q + 1) * 128],
                    func=AF.Exp, scale=cvec[:, q:q + 1], bias=cvec[:, q:q + 1]))
            E.mark("a", ("rho", wx))
            for q in range(NQ):
                E.op("a", lambda a, q=q: a.activation(
                    out=pl["pv"][:, q * 128:(q + 1) * 128],
                    in_=pl["ta"][:, q * 128:(q + 1) * 128],
                    func=AF.Exp, scale=cvec[:, 8 + q:8 + q + 1],
                    bias=cvec[:, 8 + q:8 + q + 1]))
            E.mark("a", ("rho2", wx))
        if 1 <= i <= NW:
            pwx = widx_of(g, i - 1)
            for m in range(NQ):
                E.wait("a", ("wo_ps", pwx, 2 * m + 1))
                E.op("a", lambda a, m=m: a.activation(
                    out=yfm[:, m * 128:(m + 1) * 128],
                    in_=PS[("pE" if m % 2 == 0 else "pF")][:, 0:W],
                    func=AF.Identity, bias=cvec[:, 40 + m:40 + m + 1]))
                E.mark("a", ("wo_ev", pwx, m))
                bank_release("pE" if m % 2 == 0 else "pF", ("wo_ev", pwx, m))
            E.alias(("yfm", pwx), ("wo_ev", pwx, NQ - 1))
            opar = pwx % 2
            for half in range(2):
                ob = "pE" if half == 0 else "pF"
                E.wait("a", ("ot_ps", pwx, half))
                if pwx >= 2:
                    E.wait("a", ("outw", pwx - 2))
                E.op("a", lambda a, half=half, opar=opar, ob=ob: a.copy(
                    out=otm[opar][:, half * 512:(half + 1) * 512],
                    in_=PS[ob][:, 0:512]))
                E.mark("a", ("ot_ev", pwx, half))
                bank_release(ob, ("ot_ev", pwx, half))

    def act_nrm(g, i):
        wx = widx_of(g, i)
        if 0 <= i < NW:
            par = wx % 2
            E.wait("a", ("v", wx))
            E.op("a", lambda a, par=par: a.activation(out=pl["pnrm"][par][:],
                                                      in_=pl["pv"][:], func=AF.Ln))
            E.op("a", lambda a, par=par: a.activation(out=pl["pnrm"][par][:],
                                                      in_=pl["pnrm"][par][:],
                                                      func=AF.Exp, scale=0.5))
            E.mark("a", ("nrm", wx))

    def dve_slotA(g, i):
        wx = widx_of(g, i)
        if 0 <= i + 1 < NW:
            nwx = widx_of(g, i + 1)
            p1_ = nwx % 2
            px_ = nwx % 3
            E.wait("v", ("xL", nwx))
            E.op("v", lambda v, px_=px_: v.tensor_tensor(
                out=pl["psq"][:], in0=pl["xtm"][px_][:], in1=pl["xtm"][px_][:],
                op=ALU.mult))
            E.opsync("v", lambda v, p1_=p1_, px_=px_: v.reduce_sum(
                out=lnt[:, 4 + 3 * p1_:5 + 3 * p1_], in_=pl["xtm"][px_][:], axis=AX))
            E.opsync("v", lambda v, p1_=p1_: v.reduce_sum(
                out=lnt[:, 5 + 3 * p1_:6 + 3 * p1_], in_=pl["psq"][:], axis=AX))
            E.opsync("v", lambda v, p1_=p1_: v.tensor_scalar(
                out=lnt[:, 2 * p1_:2 * p1_ + 1], in0=lnt[:, 4 + 3 * p1_:5 + 3 * p1_],
                scalar1=1.0 / D, scalar2=None, op0=ALU.mult))
            E.opsync("v", lambda v, p1_=p1_: v.tensor_scalar(
                out=lnt[:, 5 + 3 * p1_:6 + 3 * p1_], in0=lnt[:, 5 + 3 * p1_:6 + 3 * p1_],
                scalar1=1.0 / D, scalar2=None, op0=ALU.mult))
            E.opsync("v", lambda v, p1_=p1_: v.tensor_tensor(
                out=lnt[:, 11:12], in0=lnt[:, 2 * p1_:2 * p1_ + 1],
                in1=lnt[:, 2 * p1_:2 * p1_ + 1], op=ALU.mult))
            E.opsync("v", lambda v, p1_=p1_: v.tensor_tensor(
                out=lnt[:, 6 + 3 * p1_:7 + 3 * p1_], in0=lnt[:, 5 + 3 * p1_:6 + 3 * p1_],
                in1=lnt[:, 11:12], op=ALU.subtract))
            E.opsync("v", lambda v, p1_=p1_: v.tensor_scalar(
                out=lnt[:, 6 + 3 * p1_:7 + 3 * p1_], in0=lnt[:, 6 + 3 * p1_:7 + 3 * p1_],
                scalar1=1e-5, scalar2=None, op0=ALU.add))
            E.mark("v", ("var", nwx))
            E.wait("v", ("rstd", nwx))
            if not SKIP_MOE and nwx >= 2:
                E.wait("v", ("scat", nwx - 2))
            E.op("v", lambda v, p1_=p1_, px_=px_: v.tensor_scalar(
                out=pl["xntm"][p1_][:], in0=pl["xtm"][px_][:],
                scalar1=lnt[:, 2 * p1_:2 * p1_ + 1],
                scalar2=lnt[:, 2 * p1_ + 1:2 * p1_ + 2],
                op0=ALU.subtract, op1=ALU.mult))
            E.mark("v", ("xn_tm", nwx))
        if 1 <= i <= NW:
            pwx = widx_of(g, i - 1)
            par = pwx % 2
            first = (pwx % (4 * NW) == 0)
            E.wait("v", ("nrm", pwx))
            E.wait("v", ("cs", pwx))
            E.wait("v", ("sn", pwx))
            E.wait("v", ("xh_ev", pwx, NMH - 1))
            E.op("v", lambda v, par=par: v.tensor_scalar(
                out=pl["tx"][par][:], in0=pl["tx"][par][:],
                scalar1=0.5, scalar2=0.5, op0=ALU.mult, op1=ALU.add))
            E.op("v", lambda v, par=par: v.tensor_tensor(
                out=pl["tx"][par][:], in0=pl["tx"][par][:],
                in1=pl["pnrm"][par][:], op=ALU.mult))
            E.op("v", lambda v, par=par, r3=pwx % 3: v.tensor_tensor(
                out=pl["hr"][:], in0=pl["tx"][par][:],
                in1=pl["xh"][r3][:, 0:1024], op=ALU.mult))
            E.op("v", lambda v, par=par, r3=pwx % 3: v.tensor_tensor(
                out=pl["hi"][:], in0=pl["tx"][par][:],
                in1=pl["xh"][r3][:, 1024:2048], op=ALU.mult))
            E.op("v", lambda v: v.tensor_tensor(out=pl["p1"][:], in0=pl["hr"][:],
                                                in1=pl["sn"][:], op=ALU.mult))
            E.op("v", lambda v: v.tensor_tensor(out=pl["hr"][:], in0=pl["hr"][:],
                                                in1=pl["cs"][:], op=ALU.mult))
            E.op("v", lambda v: v.tensor_tensor(out=pl["p2"][:], in0=pl["hi"][:],
                                                in1=pl["sn"][:], op=ALU.mult))
            E.op("v", lambda v: v.tensor_tensor(out=pl["hi"][:], in0=pl["hi"][:],
                                                in1=pl["cs"][:], op=ALU.mult))
            E.op("v", lambda v: v.tensor_tensor(out=pl["hr"][:], in0=pl["hr"][:],
                                                in1=pl["p2"][:], op=ALU.add))
            E.op("v", lambda v: v.tensor_tensor(out=pl["hi"][:], in0=pl["hi"][:],
                                                in1=pl["p1"][:], op=ALU.subtract))
            if pwx >= 1:
                E.wait("v", ("wo_ps", pwx - 1, 2 * NQ - 1))
            if pwx >= 1:
                E.wait("v", ("hcar", pwx - 1))
            E.wait("v", ("rho", pwx))
            for q in range(NQ):
                init = 0.0 if first else carry[:, 8 + q:8 + q + 1]
                E.op("v", lambda v, q=q, init=init, par=par: v.tensor_tensor_scan(
                    out=pl["hr"][:, q * 128:(q + 1) * 128],
                    data0=pl["rho"][par][:, q * 128:(q + 1) * 128],
                    data1=pl["hr"][:, q * 128:(q + 1) * 128],
                    initial=init, op0=ALU.mult, op1=ALU.add))
            for q in range(NQ):
                init = 0.0 if first else carry[:, 16 + q:16 + q + 1]
                E.op("v", lambda v, q=q, init=init, par=par: v.tensor_tensor_scan(
                    out=pl["hi"][:, q * 128:(q + 1) * 128],
                    data0=pl["rho"][par][:, q * 128:(q + 1) * 128],
                    data1=pl["hi"][:, q * 128:(q + 1) * 128],
                    initial=init, op0=ALU.mult, op1=ALU.add))
            E.opsync("v", lambda v: v.engine_nop())
            E.opsync("v", lambda v: v.tensor_copy(out=carry[:, 8:16],
                                                  in_=pl["hr"][:, 127::128]))
            E.opsync("v", lambda v: v.tensor_copy(out=carry[:, 16:24],
                                                  in_=pl["hi"][:, 127::128]))
            E.mark("v", ("hcar", pwx))
            E.op("v", lambda v: v.tensor_tensor(out=pl["p1"][:], in0=pl["hr"][:],
                                                in1=pl["sn"][:], op=ALU.mult))
            E.op("v", lambda v: v.tensor_tensor(out=pl["p2"][:], in0=pl["hi"][:],
                                                in1=pl["sn"][:], op=ALU.mult))
            E.op("v", lambda v: v.tensor_tensor(out=pl["hr"][:], in0=pl["hr"][:],
                                                in1=pl["cs"][:], op=ALU.mult))
            E.op("v", lambda v: v.tensor_tensor(out=pl["hi"][:], in0=pl["hi"][:],
                                                in1=pl["cs"][:], op=ALU.mult))
            E.op("v", lambda v: v.tensor_tensor(
                out=pl["hm"][:, 0:1024], in0=pl["hr"][:], in1=pl["p2"][:],
                op=ALU.subtract))
            E.op("v", lambda v: v.tensor_tensor(
                out=pl["hm"][:, 1024:2048], in0=pl["hi"][:], in1=pl["p1"][:],
                op=ALU.add))
            E.mark("v", ("hm", pwx))
        if 0 <= i < NW and not SKIP_MOE:
            rpar = wx % 2
            E.wait("v", ("ltr", wx))
            E.opsync("v", lambda v: v.tensor_copy(out=ltok[:], in_=PS["pR"][:, 0:8]))
            bank_release("pR", ("ss", E._ss))
            E.opsync("v", lambda v: v.reduce_max(out=rtm[:, 0:1], in_=ltok[:], axis=AX))
            E.op("v", lambda v: v.tensor_scalar(
                out=ltok[:], in0=ltok[:], scalar1=rtm[:, 0:1], scalar2=None,
                op0=ALU.subtract))
            E.mark("v", ("rsub", wx))
            E.wait("v", ("rexp", wx))
            E.opsync("v", lambda v, rpar=rpar: v.reduce_sum(
                out=rtm[:, 1:2], in_=lp[rpar][:], axis=AX))
            E.opsync("v", lambda v: v.reciprocal(out=rtm[:, 2:3], in_=rtm[:, 1:2]))
            E.opsync("v", lambda v, rpar=rpar: v.tensor_scalar(
                out=lp[rpar][:], in0=lp[rpar][:], scalar1=rtm[:, 2:3], scalar2=None,
                op0=ALU.mult))
            E.opsync("v", lambda v, rpar=rpar: v.tensor_scalar(
                out=msk[:, 0:8], in0=lp[rpar][:], scalar1=rtm[:, 2:3], scalar2=None,
                op0=ALU.is_ge))
            E.opsync("v", lambda v, rpar=rpar: v.scalar_tensor_tensor(
                out=ltok[:], in0=msk[:, 0:8], scalar=-4.0, in1=lp[rpar][:],
                op0=ALU.mult, op1=ALU.add))
            E.opsync("v", lambda v: v.reduce_max(out=rtm[:, 3:4], in_=ltok[:], axis=AX))
            E.opsync("v", lambda v, rpar=rpar: v.tensor_scalar(
                out=msk[:, 0:8], in0=lp[rpar][:], scalar1=rtm[:, 3:4], scalar2=None,
                op0=ALU.is_ge))
            E.opsync("v", lambda v, rpar=rpar: v.tensor_tensor(
                out=lp[rpar][:], in0=lp[rpar][:], in1=msk[:, 0:8], op=ALU.mult))
            E.mark("v", ("msk_rdy", wx))
            E.wait("v", ("mtr", wx))
            E.opsync("v", lambda v: v.tensor_copy(out=mfm[0:8, 0:W],
                                                  in_=PS["pR"][0:8, 0:128]))
            bank_release("pR", ("ss", E._ss))
            first_w = (i == 0 and g % 2 == 0)
            if not first_w:
                E.wait("v", ("rcnt", wx - 1))
            if first_w:
                E.opsync("v", lambda v: v.tensor_scalar(
                    out=rcount[0:8, 0:1], in0=iota_sb[0:8, 0:1], scalar1=0.0,
                    scalar2=None, op0=ALU.is_lt))
            E.opsync("v", lambda v: v.tensor_tensor_scan(
                out=posf[0:8, 0:W], data0=mfm[0:8, 0:W], data1=mfm[0:8, 0:W],
                initial=rcount[0:8, 0:1], op0=ALU.add, op1=ALU.bypass))
            E.opsync("v", lambda v: v.tensor_copy(out=rcount[0:8, 0:1],
                                                  in_=posf[0:8, W - 1:W]))
            E.mark("v", ("rcnt", wx))
            E.opsync("v", lambda v: v.tensor_scalar(
                out=posf[0:8, 0:W], in0=posf[0:8, 0:W], scalar1=1.0, scalar2=None,
                op0=ALU.subtract))
            E.opsync("v", lambda v: v.scalar_tensor_tensor(
                out=posf[0:8, 0:W], in0=mfm[0:8, 0:W], scalar=-1e6,
                in1=posf[0:8, 0:W], op0=ALU.mult, op1=ALU.add))
            E.opsync("v", lambda v: v.tensor_scalar(
                out=posf[0:8, 0:W], in0=posf[0:8, 0:W], scalar1=1e6, scalar2=None,
                op0=ALU.add))
            E.mark("v", ("pos_rdy", wx))
            E.wait("v", ("ptr2", wx))
            E.opsync("v", lambda v: v.tensor_copy(out=idxf[:], in_=PS["pR"][:, 0:8]))
            bank_release("pR", ("ss", E._ss))
            E.opsync("v", lambda v, rpar=rpar: v.tensor_copy(
                out=idxi[rpar][:], in_=idxf[:]))
            tbase = float((g % 4) * TG + i * W)
            E.opsync("v", lambda v, rpar=rpar, tbase=tbase: v.tensor_scalar(
                out=tokv[rpar][:], in0=iota_sb[:], scalar1=tbase, scalar2=None,
                op0=ALU.add))
            for e in range(NE):
                E.op("v", lambda v, e=e, rpar=rpar: v.tensor_copy(
                    out=T["lpt"][rpar][:, 2 * e:2 * e + 1], in_=lp[rpar][:, e:e + 1]))
                E.op("v", lambda v, e=e, rpar=rpar: v.tensor_copy(
                    out=T["lpt"][rpar][:, 2 * e + 1:2 * e + 2], in_=tokv[rpar][:]))
            E.opsync("v", lambda v: v.engine_nop())
            E.mark("v", ("idx", wx))
        if 0 <= i < NW:
            par = wx % 2
            first = (wx % (4 * NW) == 0)
            E.wait("v", ("g_ev", wx, 1, NQ - 1))
            E.wait("v", "cvec")
            if i >= 1 or g >= 1:
                pw2 = wx - 1
                E.wait("v", ("cs", pw2))
            for q in range(NQ):
                E.op("v", lambda v, q=q: v.tensor_scalar(
                    out=pl["p1"][:, q * 128:(q + 1) * 128],
                    in0=pl["ta"][:, q * 128:(q + 1) * 128],
                    scalar1=cvec[:, 16 + q:16 + q + 1],
                    scalar2=cvec[:, 16 + q:16 + q + 1],
                    op0=ALU.mult, op1=ALU.add))
            if wx >= 1:
                E.wait("v", ("phic", wx - 1))
            for q in range(NQ):
                init = 0.0 if first else carry[:, q:q + 1]
                E.op("v", lambda v, q=q, init=init: v.tensor_tensor_scan(
                    out=pl["p2"][:, q * 128:(q + 1) * 128],
                    data0=pl["p1"][:, q * 128:(q + 1) * 128],
                    data1=pl["p1"][:, q * 128:(q + 1) * 128],
                    initial=init, op0=ALU.add, op1=ALU.bypass))
            E.opsync("v", lambda v: v.engine_nop())
            E.opsync("v", lambda v: v.tensor_copy(out=carry[:, 0:8],
                                                  in_=pl["p2"][:, 127::128]))
            E.mark("v", ("phic", wx))
            def range_reduce(dst):
                for h in range(2):
                    E.op("v", lambda v, h=h, dst=dst: v.tensor_copy(
                        out=pl["pi"][:], in_=pl[dst][:, h * 512:(h + 1) * 512]))
                    E.op("v", lambda v, h=h: v.tensor_copy(
                        out=pl["hr"][:, h * 512:(h + 1) * 512], in_=pl["pi"][:]))
                E.op("v", lambda v, dst=dst: v.scalar_tensor_tensor(
                    out=pl[dst][:], in0=pl["hr"][:], scalar=-1.0, in1=pl[dst][:],
                    op0=ALU.mult, op1=ALU.add))
                E.op("v", lambda v, dst=dst: v.tensor_scalar(
                    out=pl["hr"][:], in0=pl[dst][:], scalar1=0.0, scalar2=None,
                    op0=ALU.is_lt))
                E.op("v", lambda v, dst=dst: v.tensor_scalar(
                    out=pl[dst][:], in0=pl[dst][:], scalar1=0.5, scalar2=None,
                    op0=ALU.subtract))
                E.op("v", lambda v, dst=dst: v.tensor_tensor(
                    out=pl[dst][:], in0=pl[dst][:], in1=pl["hr"][:], op=ALU.add))
            E.op("v", lambda v: v.tensor_copy(out=pl["p1"][:], in_=pl["p2"][:]))
            range_reduce("p1")
            E.mark("v", ("fr", wx))
            E.op("v", lambda v: v.tensor_scalar(out=pl["p2"][:], in0=pl["p2"][:],
                                                scalar1=0.25, scalar2=None,
                                                op0=ALU.add))
            range_reduce("p2")
            E.mark("v", ("fr2", wx))
            E.wait("v", ("rho2", wx))
            E.op("v", lambda v: v.tensor_scalar(out=pl["pv"][:], in0=pl["pv"][:],
                                                scalar1=-1.0, scalar2=1.0,
                                                op0=ALU.mult, op1=ALU.add))
            E.op("v", lambda v: v.tensor_scalar(out=pl["pv"][:], in0=pl["pv"][:],
                                                scalar1=1e-6, scalar2=None,
                                                op0=ALU.max))
            E.mark("v", ("v", wx))
        if 1 <= i <= NW:
            pwx = widx_of(g, i - 1)
            opar = pwx % 2
            pxr = pwx % 3
            E.wait("v", ("ot_ev", pwx, 1))
            E.op("v", lambda v, opar=opar, pxr=pxr: v.tensor_tensor(
                out=otm[opar][:], in0=otm[opar][:],
                in1=pl["xtm"][pxr][:], op=ALU.add))
            E.mark("v", ("osum", pwx))

    def g_slotA(g, i):
        wx = widx_of(g, i)
        seq = g // 4
        pq = (g // 2) % 2
        if i == -1 and g % 2 == 0 and not SKIP_MOE:
            pr = g // 2
            if pr >= 2:
                E.wait("g", ("ptL", pr - 2, NE - 1))
            E.dma("g", lambda gg, pq=pq: gg.dma_start(
                out=T["ptbuf"].ap()[pq * NE * PCAP:(pq + 1) * NE * PCAP, :],
                in_=T["bigc_d"]))
            E.dmark("g", ("ptfill", pr))
        if 1 <= i <= NW:
            pwx = widx_of(g, i - 1)
            opar = pwx % 2
            t0 = (g % 4) * TG + (i - 1) * W
            E.wait("s", ("osum", pwx))
            E.dma("s", lambda ss, opar=opar, t0=t0: ss.dma_start(
                out=out_d[seq, t0:t0 + W, :], in_=otm[opar][:]))
            E.dmark("s", ("outw", pwx))
        if 0 <= i < NW and not SKIP_MOE:
            rpar = wx % 2
            E.wait("g", ("idx", wx))
            E.wait("g", ("xn_tm", wx))
            E.wait("g", ("ptfill", g // 2))
            if g // 2 >= 2:
                E.wait("g", ("xeL", g // 2 - 2, NE - 1, PCT - 1))
            for e in range(NE):
                boff = (pq * NE + e) * PCAP
                E.dma("c", lambda gg, e=e, boff=boff, rpar=rpar: gg.indirect_dma_start(
                    out=T["xbuf"].ap(),
                    out_offset=bass.IndirectOffsetOnAxis(
                        ap=idxi[rpar][:, e:e + 1], axis=0),
                    in_=pl["xntm"][rpar][:], in_offset=None,
                    element_offset=boff * D,
                    bounds_check=REG["cap"], oob_is_err=False))
                E.dma("c", lambda gg, e=e, boff=boff, rpar=rpar: gg.indirect_dma_start(
                    out=T["ptbuf"].ap(),
                    out_offset=bass.IndirectOffsetOnAxis(
                        ap=idxi[rpar][:, e:e + 1], axis=0),
                    in_=T["lpt"][rpar][:, 2 * e:2 * e + 2], in_offset=None,
                    element_offset=boff * 2,
                    bounds_check=REG["cap"], oob_is_err=False))
            E.dbar("c", ("scat", wx))

    def pair_lastwx(p):
        return widx_of(2 * p + 1, NW - 1)

    def moe_loads_x(p, e, trange=None):
        pq = p % 2
        u_e = p * NE + e
        for t in (range(PCT) if trange is None else trange):
            tau = u_e * PCT + t
            if tau >= 4:
                uu = tau - 4
                E.wait(MLQ, ("xeT", uu // PCT // NE, (uu // PCT) % NE, uu % PCT))
            E.wait(MLQ, ("scat", pair_lastwx(p)))
            E.dma(MLQ, lambda gg, e=e, t=t, pq=pq, bi=(u_e * PCT + t) % 4:
                  gg.dma_start(
                      out=xetm[bi][:],
                in_=T["xbuf"].ap()[(pq * NE + e) * PCAP + t * 128:
                                   (pq * NE + e) * PCAP + (t + 1) * 128, :]))
            E.dmark(MLQ, ("xeL", p, e, t))
        if trange is not None and trange.start != 0:
            return
        if u_e >= 2:
            E.wait(MLQ, ("sidx", *divmod(u_e - 2, NE)))
            E.wait(MLQ, ("oev", *divmod(u_e - 2, NE), PCT - 1, 1))
        E.dma(MLQ, lambda gg, e=e, pq=pq, pp=u_e % 2: gg.dma_start(
            out=pslot[pp][:],
            in_=T["ptbuf"].ap()[(pq * NE + e) * PCAP:(pq * NE + e + 1) * PCAP,
                                :].rearrange("(t p) c -> p t c", p=128)))
        E.dmark(MLQ, ("ptL", p, e))

    def _yload(p, e, j):
        u_e = p * NE + e
        u1 = u_e * NJ + j
        slot = u1 % 2
        if u1 >= 2:
            uu = u1 - 2
            E.wait(YQ, ("y1ps", uu // NJ // NE, (uu // NJ) % NE, uu % NJ, 1))
        E.wait(YQ, ("cast_w1", e))
        E.dma(YQ, lambda gg, e=e, j=j, slot=slot: gg.dma_start(
            out=y1t[:, slot * 1024:(slot + 1) * 1024],
            in_=T["w1bf"].ap()[e, j].rearrange("p k c -> p (k c)")))
        E.dmark(YQ, ("y1L", p, e, j))
        if u1 >= 2:
            uu = u1 - 2
            E.wait(YQ, ("y2ps", uu // NJ // NE, (uu // NJ) % NE, uu % NJ, 1))
        E.dma(YQ, lambda gg, e=e, j=j, slot=slot: gg.dma_start(
            out=y2t[:, slot * 1024:(slot + 1) * 1024],
            in_=T["w1bf"].ap()[e, NJ + j].rearrange("p k c -> p (k c)")))
        E.dmark(YQ, ("y2L", p, e, j))

    def moe_loads_w_head(p, e):
        u_e = p * NE + e
        if u_e >= 1:
            E.wait(MLQ, ("o_ps", *divmod(u_e - 1, NE), PCT - 1, 1))
        E.wait(MLQ, ("cast_w2", e))
        E.dma(MLQ, lambda gg, e=e: gg.dma_start(
            out=w2rt[:], in_=T["w2bf"].ap()[e].rearrange("j p c -> p j c")))
        E.dmark(MLQ, ("w2rL", p, e))

    def moe_y_head(p, e):
        for j in range(2):
            _yload(p, e, j)

    def moe_y_tail(p, e):
        for j in range(2, NJ):
            _yload(p, e, j)

    def moe_expert(p, e):
        pq = p % 2
        seq = p // 2
        lastwx = pair_lastwx(p)
        if True:
            u_e = p * NE + e

            for t in range(PCT):
                E.wait("t", ("xeL", p, e, t))
                bank_acquire("pT", "t")
                for q in range(NQ):
                    E.op("t", lambda tt, q=q, bi=(u_e * PCT + t) % 4:
                         tt.transpose(
                             out=PS["pT"][:, q * 128:(q + 1) * 128],
                             in_=xetm[bi][:, q * 128:(q + 1) * 128],
                             identity=identb[:]))
                E.mark("t", ("xeT", p, e, t))
                E.wait("a", ("xeT", p, e, t))
                E.op("a", lambda a, t=t: a.copy(
                    out=xef[:].rearrange("p (q c) -> p q c", q=NQ)[
                        :, :, t * 128:(t + 1) * 128],
                    in_=PS["pT"][:, 0:1024]))
                E.mark("a", ("xeE", p, e, t))
                bank_release("pT", ("xeE", p, e, t))

            E.wait("v", ("ptL", p, e))
            if u_e >= 1:
                E.wait("v", ("oadd", *divmod(u_e - 1, NE), PCT - 1))
            E.opsync("v", lambda v, pp=u_e % 2: v.tensor_copy(
                out=idxf[:, 0:PCT], in_=pslot[pp][:, 1::2]))
            E.opsync("v", lambda v: v.tensor_copy(out=sidxi[:], in_=idxf[:, 0:PCT]))
            E.mark("v", ("sidx", p, e))

            for j in range(NJ):
                for c in range(2):
                    c0, cw = c * 512, CW[c]
                    uc = u_e * NJ * 2 + j * 2 + c
                    b1 = "pA" if uc % 2 == 0 else "pB"
                    b2 = "pC" if uc % 2 == 0 else "pD"
                    sl = uc % 2
                    slot = (u_e * NJ + j) % 2
                    E.wait("t", ("y1L", p, e, j))
                    E.wait("t", ("xeE", p, e, PCT - 1))
                    bank_acquire(b1, "t")
                    for k in range(NQ):
                        E.op("t", lambda tt, k=k, b1=b1, slot=slot, c0=c0, cw=cw:
                             tt.matmul(
                                 out=PS[b1][:, 0:cw],
                                 lhsT=y1t[:, (slot * 8 + k) * 128:
                                          (slot * 8 + k + 1) * 128],
                                 rhs=xef[:, k * PCAP + c0:k * PCAP + c0 + cw],
                                 start=(k == 0), stop=(k == NQ - 1)))
                    E.mark("t", ("y1ps", p, e, j, c))
                    E.wait("t", ("y2L", p, e, j))
                    bank_acquire(b2, "t")
                    for k in range(NQ):
                        E.op("t", lambda tt, k=k, b2=b2, slot=slot, c0=c0, cw=cw:
                             tt.matmul(
                                 out=PS[b2][:, 0:cw],
                                 lhsT=y2t[:, (slot * 8 + k) * 128:
                                          (slot * 8 + k + 1) * 128],
                                 rhs=xef[:, k * PCAP + c0:k * PCAP + c0 + cw],
                                 start=(k == 0), stop=(k == NQ - 1)))
                    E.mark("t", ("y2ps", p, e, j, c))
                    E.wait("a", ("y1ps", p, e, j, c))
                    if uc >= 2:
                        uu = uc - 2
                        E.wait("a", ("g1", uu // 2 // NJ // NE,
                                     (uu // 2 // NJ) % NE, (uu // 2) % NJ,
                                     uu % 2))
                    E.op("a", lambda a, sl=sl, b1=b1, cw=cw: a.activation(
                        out=silu_sb[:, sl * 512:sl * 512 + cw],
                        in_=PS[b1][:, 0:cw], func=AF.Silu))
                    E.mark("a", ("sil", p, e, j, c))
                    bank_release(b1, ("sil", p, e, j, c))
                    E.wait("v", ("sil", p, e, j, c))
                    E.wait("v", ("y2ps", p, e, j, c))
                    if u_e >= 1:
                        E.wait("v", ("o_ps", *divmod(u_e - 1, NE), PCT - 1, 1))
                    E.op("v", lambda v, j=j, sl=sl, b2=b2, c0=c0, cw=cw:
                         v.tensor_tensor(
                             out=gmoe[:, j * PCAP + c0:j * PCAP + c0 + cw],
                             in0=silu_sb[:, sl * 512:sl * 512 + cw],
                             in1=PS[b2][:, 0:cw], op=ALU.mult))
                    E.mark("v", ("g1", p, e, j, c))
                    bank_release(b2, ("g1", p, e, j, c))

            for st in range(PCT):
                for half in range(2):
                    bo = "pE" if (st * 2 + half) % 2 == 0 else "pF"
                    E.wait("t", ("w2rL", p, e))
                    E.wait("t", ("g1", p, e, NJ - 1, 1))
                    bank_acquire(bo, "t")
                    for j in range(NJ):
                        E.op("t", lambda tt, j=j, st=st, half=half, bo=bo: tt.matmul(
                            out=PS[bo][:, 0:512],
                            lhsT=gmoe[:, j * PCAP + st * 128: j * PCAP + st * 128 + 128],
                            rhs=w2rt[:, j * 1024 + half * 512: j * 1024 + (half + 1) * 512],
                            start=(j == 0), stop=(j == NJ - 1)))
                    E.mark("t", ("o_ps", p, e, st, half))
                    E.wait("a", ("o_ps", p, e, st, half))
                    E.wait("a", ("ptL", p, e))
                    if u_e * PCT + st >= 2:
                        uu = u_e * PCT + st - 2
                        E.wait("a", ("oadd", uu // PCT // NE, (uu // PCT) % NE, uu % PCT))
                    E.op("a", lambda a, st=st, half=half, bo=bo, pp=u_e % 2:
                         a.activation(
                             out=o_tok[st % 2][:, half * 512:(half + 1) * 512],
                             in_=PS[bo][:, 0:512], func=AF.Identity,
                             scale=pslot[pp][:, 2 * st:2 * st + 1]))
                    E.mark("a", ("oev", p, e, st, half))
                    bank_release(bo, ("oev", p, e, st, half))
                E.wait("g", ("oev", p, e, st, 1))
                E.wait("g", ("sidx", p, e))
                E.wait("g", ("outw", lastwx))
                if st == 0 and u_e >= 1:
                    for st_ in range(PCT):
                        E.wait("g", ("oadd", *divmod(u_e - 1, NE), st_))
                E.dma("o", lambda gg, st=st, seq=seq: gg.indirect_dma_start(
                    out=out_d.rearrange("b s d -> (b s) d"),
                    out_offset=bass.IndirectOffsetOnAxis(
                        ap=sidxi[:, st:st + 1], axis=0),
                    in_=o_tok[st % 2][:], in_offset=None,
                    element_offset=seq * S * D,
                    bounds_check=REG["tok"], oob_is_err=False,
                    compute_op=ALU.add))
                E.dmark("o", ("oadd", p, e, st))

    NPAIR = NG_RUN // 2
    for g in range(NG_RUN):
        if g == 0:
            sync_loadx(0, 0)
        for i in range(-1, NW + 1):
            pr = g // 2
            il = not SKIP_MOE and pr >= 1 and pr - 1 < IL_UPTO
            s20 = (g % 2) * 10 + (i + 1)
            nwx2 = widx_of(g, i) + 2
            if (0 <= i < NW or (g == 0 and i == -1)) and nwx2 < NG_RUN * NW:
                sync_loadx(nwx2 // NW, nwx2 % NW)
            ev = il and s20 % 2 == 0 and 2 <= s20 <= 18
            m = s20 // 2 - 1 if ev else None
            if ev and m <= NE - 1:
                moe_loads_x(pr - 1, m, range(0, 4))
            sync_slotA(g, i)
            if ev and m <= NE - 1:
                moe_loads_w_head(pr - 1, m)
            sync_woL(g, i)
            if il and s20 % 2 == 1 and 3 <= s20 <= 17:
                moe_loads_x(pr - 1, (s20 - 3) // 2, range(4, PCT))
            pe_slotA(g, i)
            act_slotA(g, i)
            dve_slotA(g, i)
            g_slotA(g, i)
            if ev:
                if m >= 1:
                    moe_y_tail(pr - 1, m - 1)
                if m <= NE - 1:
                    moe_y_head(pr - 1, m)
            if ev and m >= 1:
                moe_expert(pr - 1, m - 1)
            act_nrm(g, i)
    if not SKIP_MOE:
        for p in range(min(IL_UPTO, NPAIR - 1), NPAIR):
            moe_loads_x(p, 0)
            moe_loads_w_head(p, 0)
            moe_y_head(p, 0)
            for e in range(NE):
                moe_y_tail(p, e)
                moe_expert(p, e)
                if e + 1 < NE:
                    moe_loads_x(p, e + 1)
                    moe_loads_w_head(p, e + 1)
                    moe_y_head(p, e + 1)

    if DEBUG:
        E.wait("s", ("outw", widx_of(NG_RUN - 1, NW - 1)))
        for nm in ("cvec", "lnt", "carry", "xng"):
            E.dma("s", lambda s, nm=nm: s.dma_start(
                out=T["dbg_" + nm], in_=T[nm][:]))
        E.dma("s", lambda s: s.dma_start(out=T["dbg_ta"], in_=T["pl"]["ta"][:]))
        for nm in ("tx", "rho"):
            for p_ in range(2):
                E.dma("s", lambda s, nm=nm, p_=p_: s.dma_start(
                    out=T[f"dbg_{nm}{p_}"], in_=T["pl"][nm][p_][:]))
        for nm in ("hm", "cs", "sn", "p1", "p2", "hr", "hi", "pv", "pnrm0",
                   "pnrm1", "xh0", "xh1"):
            if nm in ("pnrm0", "pnrm1"):
                src = T["pl"]["pnrm"][int(nm[-1])]
            elif nm in ("xh0", "xh1"):
                src = T["pl"]["xh"][int(nm[-1])]
            else:
                src = T["pl"][nm]
            E.dma("s", lambda s, nm=nm, src=src: s.dma_start(
                out=T["dbg_" + nm], in_=src[:]))
        E.drain_dma("s", "s")

    E.wait("s", ("outw", widx_of(NG_RUN - 1, NW - 1)))
    E.wait("v", ("outw", widx_of(NG_RUN - 1, NW - 1)))
    E.drain_dma("s", "g")
    E.drain_dma("s", "s")
    E.drain_dma("s", "o")
    E.drain_dma("v", "g")
    E.drain_dma("v", "o")



def _build_nc():
    nc = bass.Bass("TRN2", target_bir_lowering=False)
    T = {}
    T["x"] = nc.dram_tensor("x", [2, S, D], F32, kind="ExternalInput").ap()
    T["out"] = nc.dram_tensor("out", [2, S, D], F32, kind="ExternalOutput").ap()
    T["w_in_d"] = nc.dram_tensor("w_in_t", [NMH, 128, NQ, 128], F32, kind="ExternalInput").ap()
    T["w_out_d"] = nc.dram_tensor("w_out_t", [NQ, 128, NMH, 128], F32, kind="ExternalInput").ap()
    T["w_gx_d"] = nc.dram_tensor("w_gx_t", [128, 8, 2, 128], F32, kind="ExternalInput").ap()
    T["w_ga_d"] = nc.dram_tensor("w_ga_t", [128, 8, 2, 128], F32, kind="ExternalInput").ap()
    T["w1_d"] = nc.dram_tensor("w1_t", [NE, 22, 128, NQ, 128], F32, kind="ExternalInput").ap()
    T["w2_d"] = nc.dram_tensor("w2_t", [NE, NJ, 128, 1024], F32, kind="ExternalInput").ap()
    T["rw_d"] = nc.dram_tensor("rw_t", [128, NQ, 8], F32, kind="ExternalInput").ap()
    T["cin_d"] = nc.dram_tensor("cin", [128, 32], F32, kind="ExternalInput").ap()
    T["binv_d"] = nc.dram_tensor("binv", [128, 16], F32, kind="ExternalInput").ap()
    T["boutv_d"] = nc.dram_tensor("boutv", [128, 8], F32, kind="ExternalInput").ap()
    T["identf_d"] = nc.dram_tensor("identf", [128, 128], F32, kind="ExternalInput").ap()
    T["identb_d"] = nc.dram_tensor("identb", [128, 128], BF16, kind="ExternalInput").ap()
    T["iota_d"] = nc.dram_tensor("iota", [128, 1], F32, kind="ExternalInput").ap()
    T["bigc_d"] = nc.dram_tensor("bigc", [NE, PCAP, 2], F32, kind="ExternalInput").ap()
    if DEBUG:
        dbg_specs = [("cvec", [128, 64], F32), ("lnt", [128, 12], F32),
                     ("carry", [128, 24], F32), ("ta", [128, 1024], F32),
                     ("xng", [128, NQ * TG], BF16),
                     ("tx0", [128, 1024], F32), ("tx1", [128, 1024], F32),
                     ("rho0", [128, 1024], F32), ("rho1", [128, 1024], F32),
                     ("hm", [128, 2048], BF16), ("cs", [128, 1024], F32),
                     ("sn", [128, 1024], F32), ("p1", [128, 1024], F32),
                     ("p2", [128, 1024], F32), ("hr", [128, 1024], F32),
                     ("hi", [128, 1024], F32), ("pv", [128, 1024], F32),
                     ("pnrm0", [128, 1024], F32), ("pnrm1", [128, 1024], F32),
                     ("xh0", [128, 2048], BF16), ("xh1", [128, 2048], BF16)]
        for nm, shp, dt in dbg_specs:
            T["dbg_" + nm] = nc.dram_tensor("dbg_" + nm, shp, dt,
                                            kind="ExternalOutput").ap()
    T["wibf"] = nc.dram_tensor("wibf", [NMH, 128, NQ, 128], BF16, kind="Internal")
    T["wobf"] = nc.dram_tensor("wobf", [NQ, 128, NMH, 128], BF16, kind="Internal")
    T["w1bf"] = nc.dram_tensor("w1bf", [NE, 22, 128, NQ, 128], BF16, kind="Internal")
    T["w2bf"] = nc.dram_tensor("w2bf", [NE, NJ, 128, 1024], BF16, kind="Internal")
    T["xbuf"] = nc.dram_tensor("xbuf", [2 * NE * PCAP, D], BF16, kind="Internal")
    T["ptbuf"] = nc.dram_tensor("ptbuf", [2 * NE * PCAP, 2], F32, kind="Internal")

    with contextlib.ExitStack() as ctx:
        def sb(name, shape, dt):
            return ctx.enter_context(nc.sbuf_tensor(name, shape, dt))

        pl = {}
        pl["xtm"] = [sb("xtm0", [128, 1024], F32), sb("xtm1", [128, 1024], F32),
                     sb("xtm2", [128, 1024], F32)]
        pl["xntm"] = [sb("xntm0", [128, 1024], BF16), sb("xntm1", [128, 1024], BF16)]
        pl["xh"] = [sb("xh0", [128, 2048], BF16), sb("xh1", [128, 2048], BF16),
                    sb("xh2", [128, 2048], BF16)]
        pl["tx"] = [sb("tx0", [128, 1024], F32), sb("tx1", [128, 1024], F32)]
        pl["ta"] = sb("ta", [128, 1024], F32)
        pl["rho"] = [sb("rho0", [128, 1024], F32), sb("rho1", [128, 1024], F32)]
        pl["pnrm"] = [sb("pnrm0", [128, 1024], F32), sb("pnrm1", [128, 1024], F32)]
        pl["hm"] = sb("hm", [128, 2048], BF16)
        for nm in ("pv", "p1", "p2", "cs", "sn", "hr", "hi"):
            pl[nm] = sb(nm, [128, 1024], F32)
        pl["psq"] = pl["hr"]
        pl["pi"] = sb("pi", [128, 512], I32)
        T["pl"] = pl

        T["cvec"] = sb("cvec", [128, 64], F32)
        T["cin"] = sb("cins", [128, 32], F32)
        T["binv"] = sb("binvs", [128, 16], F32)
        T["lnt"] = sb("lnt", [128, 12], F32)
        T["carry"] = sb("carry", [128, 24], F32)
        T["xng"] = sb("xng", [128, NQ * 2 * W], BF16)
        T["gmoe"] = sb("gmoe", [128, NJ * PCAP], BF16)
        T["wgates"] = sb("wgates", [128, 4096], BF16)
        T["rw_sb"] = sb("rw_sb", [128, 64], BF16)
        T["win_sb"] = sb("win_sb", [128, 3072], BF16)
        T["wout_sb"] = sb("wout_sb", [128, 4096], BF16)
        T["y1t"] = sb("y1t", [128, 2048], BF16)
        T["y2t"] = sb("y2t", [128, 2048], BF16)
        T["w2rt"] = sb("w2rt", [128, NJ * 1024], BF16)
        T["silu_sb"] = sb("silu_sb", [128, 768], BF16)
        T["identf"] = sb("identfs", [128, 128], F32)
        T["identb"] = sb("identbs", [128, 128], BF16)
        T["iota_sb"] = sb("iota_sb", [128, 1], F32)
        T["rlg"] = sb("rlg", [8, 128], F32)
        T["ltok"] = sb("ltokt", [128, 8], F32)
        T["lp"] = [sb("lp0", [128, 8], F32), sb("lp1", [128, 8], F32)]
        T["msk"] = sb("msk", [128, 8], F32)
        T["rtm"] = sb("rtm", [128, 4], F32)
        T["mfm"] = sb("mfm", [8, 128], F32)
        T["posf"] = sb("posf", [8, 128], F32)
        T["rcount"] = sb("rcount", [8, 1], F32)
        T["idxf"] = sb("idxf", [128, 8], F32)
        T["idxi"] = [sb("idxi0", [128, 8], I32), sb("idxi1", [128, 8], I32)]
        T["tokv"] = [sb("tokv0", [128, 1], F32), sb("tokv1", [128, 1], F32)]
        T["lpt"] = [sb("lpt0", [128, 16], F32), sb("lpt1", [128, 16], F32)]
        T["xef"] = sb("xef", [128, NQ * PCAP], BF16)
        T["xetm"] = [sb(f"xetm{i}", [128, 1024], BF16) for i in range(4)]
        T["o_tok"] = [sb("o_tok0", [128, 1024], F32), sb("o_tok1", [128, 1024], F32)]
        T["pslot"] = [sb("pslot0", [128, 2 * PCT], F32),
                      sb("pslot1", [128, 2 * PCT], F32)]
        T["sidxi"] = sb("sidxi", [128, PCT], I32)
        T["yfm"] = sb("yfm", [128, 1024], F32)
        T["otm"] = [sb("otm0", [128, 1024], F32), sb("otm1", [128, 1024], F32)]

        PS = {}
        for nm in ("pA", "pB", "pC", "pD", "pE", "pF", "pR"):
            PS[nm] = ctx.enter_context(nc.psum_tensor(nm, [128, 512], F32))
        PS["pT"] = ctx.enter_context(nc.psum_tensor("pT", [128, 1024], BF16))
        T["ps"] = PS

        sems = {}
        for stname in ("t", "v", "a"):
            sems[stname] = ctx.enter_context(nc.semaphore("sem_" + stname))
        for stname, nl in DMA_LANES.items():
            for lane in range(nl):
                key = stname + str(lane)
                sems[key] = ctx.enter_context(nc.semaphore("sem_" + key))
        block = ctx.enter_context(nc.Block())

        E1 = Em(1)
        _emit(E1, T)
        marks = E1.marks

        def section(stname):
            def run(engine):
                E2 = Em(2, marks)
                E2.active = stname
                E2.engines = {stname: engine}
                E2.sems = sems
                _emit(E2, T)
            return run

        block.sync(section("s"))
        block.gpsimd(section("g"))
        block.tensor(section("t"))
        block.scalar(section("a"))
        block.vector(section("v"))

    return nc


def _elide_ldweights(nc):
    n_elided = 0
    for f in nc.m.functions:
        for bb in f.blocks:
            insl = bb.instructions
            prev_key = None
            for k, ins in enumerate(insl):
                tn = type(ins).__name__
                if tn == "InstMatmult":
                    if ins.is_transpose or ins.ins[1].dtype != BF16 \
                            or ins.ins[0].dtype != BF16:
                        prev_key = None
                        continue
                    key = (str(ins.ins[1]), str(ins.tile_position),
                           str(ins.perf_mode))
                    if prev_key is not None and key == prev_key:
                        insl[k] = dataclasses.replace(ins, ldweights=False)
                        n_elided += 1
                    prev_key = key
                elif tn in ("InstEventSemaphore", "InstNop"):
                    continue
                else:
                    prev_key = None
    return n_elided


_NC_CACHE = None


def kernel(**inputs):
    global _NC_CACHE
    x, com = _prep(inputs)
    if _NC_CACHE is None:
        _NC_CACHE = _build_nc()
        _elide_ldweights(_NC_CACHE)
    nc = _NC_CACHE
    in_maps = []
    for c in range(NCORES):
        m = dict(com)
        m["x"] = np.ascontiguousarray(x[2 * c:2 * c + 2])
        in_maps.append(m)
    res = run_bass_kernel_spmd(nc, in_maps, core_ids=list(range(NCORES)))
    out = np.empty((16, S, D), dtype=np.float32)
    for c in range(NCORES):
        out[2 * c:2 * c + 2] = np.asarray(res.results[c]["out"])
    return out


# version 31 (final)
# speedup vs baseline: 1.0074x; 1.0074x over previous
import contextlib
import dataclasses
import math

import numpy as np
import ml_dtypes

import concourse.bass as bass
import concourse.mybir as mybir
from concourse.bass_utils import run_bass_kernel_spmd

F32 = mybir.dt.float32
BF16 = mybir.dt.bfloat16
I32 = mybir.dt.int32
AF = mybir.ActivationFunctionType
ALU = mybir.AluOpType
AX = mybir.AxisListType.X

NCORES = 8
S = 4096
D = 1024
DH = 2048
NQ = 8
NMH = 16
NE = 8
FF = 1408
NJ = 11
TG = 1024
NG = 8
W = 128
PCAP = 768
PCT = PCAP // 128
CW = (512, 256)
NW = TG // W
NG_RUN = NG
MLQ = "s"
YQ = "g"
IL_UPTO = NG
DEBUG = False
SKIP_MOE = False
INV4PI = 1.0 / (4.0 * math.pi)
TWO_PI = 2.0 * math.pi


STREAMS = ("t", "v", "a", "s", "g")


DMA_LANES = {"s": 16, "g": 8, "o": 6, "c": 6}
DMA_ENG = {"s": "s", "g": "g", "o": "g", "c": "g"}


class Em:

    def __init__(self, phase, marks=None):
        self.phase = phase
        self.active = None
        self.engines = {}
        self.sems = {}
        self.cnt = {k: 0 for k in ("t", "v", "a")}
        self.dma_idx = {k: 0 for k in DMA_LANES}
        self.lane_cnt = {k: [0] * n for k, n in DMA_LANES.items()}
        self.last_dma = {k: None for k in DMA_LANES}
        self.marks = {} if phase == 1 else marks
        self.last = {k: None for k in ("t", "v", "a")}
        self.seen = {k: {} for k in STREAMS}

    def op(self, st, fn):
        if self.phase == 2 and self.active == st:
            self.last[st] = fn(self.engines[st])

    def dma(self, st, fn):
        eng_st = DMA_ENG[st]
        i = self.dma_idx[st]
        self.dma_idx[st] = i + 1
        lane = i % DMA_LANES[st]
        key = st + str(lane)
        prev = self.lane_cnt[st][lane] * 16
        self.lane_cnt[st][lane] += 1
        self.last_dma[st] = (key, prev + 16)
        if self.phase == 2 and self.active == eng_st:
            eng = self.engines[eng_st]
            if prev > 0 and self.seen[eng_st].get(key, -1) < prev:
                eng.wait_ge(self.sems[key], prev)
                self.seen[eng_st][key] = prev
            inst = fn(eng)
            inst.then_inc(self.sems[key], 16)

    def dmark(self, st, label):
        if self.phase == 1:
            assert label not in self.marks, label
            self.marks[label] = self.last_dma[st]

    def dbar(self, dma_st, label):
        if self.phase == 1:
            assert label not in self.marks, label
            pairs = [(dma_st + str(ln), c * 16)
                     for ln, c in enumerate(self.lane_cnt[dma_st]) if c > 0]
            self.marks[label] = ("MULTI", pairs)

    def mark(self, st, label):
        self.cnt[st] += 1
        if self.phase == 1:
            assert label not in self.marks, label
            self.marks[label] = (st, self.cnt[st])
        elif self.active == st:
            li = self.last[st]
            if li is None:
                li = self.engines[st].nop()
            li.then_inc(self.sems[st], 1)
            self.last[st] = None

    def alias(self, label, src):
        if self.phase == 1:
            assert label not in self.marks, label
            self.marks[label] = self.marks[src]

    def wait(self, st, label):
        if self.phase == 1:
            return
        prod, val = self.marks[label]
        if self.active != st:
            return
        pairs = val if prod == "MULTI" else [(prod, val)]
        for p_, v_ in pairs:
            if self.seen[st].get(p_, -1) >= v_:
                continue
            self.engines[st].wait_ge(self.sems[p_], v_)
            self.seen[st][p_] = v_

    def opsync(self, st, fn):
        self.op(st, fn)
        self._ss = getattr(self, "_ss", 0) + 1
        lbl = ("ss", self._ss)
        self.mark(st, lbl)
        self.wait(st, lbl)

    def drain_dma(self, st, dma_st):
        if self.phase == 1:
            return
        if self.active != st:
            return
        for lane in range(DMA_LANES[dma_st]):
            key = dma_st + str(lane)
            val = self.lane_cnt[dma_st][lane] * 16
            if val > 0 and self.seen[st].get(key, -1) < val:
                self.engines[st].wait_ge(self.sems[key], val)
                self.seen[st][key] = val



def _vec_tile(v, n):
    return np.ascontiguousarray(v.reshape(n, 128).T)


def _prep(kw):
    x = np.asarray(kw["L0_attn_x"], dtype=np.float32)
    w_in = np.asarray(kw["L0_attn_w_in"], np.float32)
    b_in = np.asarray(kw["L0_attn_b_in"], np.float32)
    w_gx = np.asarray(kw["L0_attn_w_gx"], np.float32)
    b_gx = np.asarray(kw["L0_attn_b_gx"], np.float32)
    w_ga = np.asarray(kw["L0_attn_w_ga"], np.float32)
    b_ga = np.asarray(kw["L0_attn_b_ga"], np.float32)
    arp = np.asarray(kw["L0_attn_a_real_param"], np.float32)
    aip = np.asarray(kw["L0_attn_a_imag_param"], np.float32)
    w_out = np.asarray(kw["L0_attn_w_out"], np.float32)
    b_out = np.asarray(kw["L0_attn_b_out"], np.float32)
    rw = np.asarray(kw["L0_ffn_router_weight"], np.float32)
    w1 = np.asarray(kw["L0_ffn_w1"], np.float32)
    w2 = np.asarray(kw["L0_ffn_w2"], np.float32)

    com = {}
    com["w_in_t"] = np.ascontiguousarray(
        w_in.reshape(8, 128, 16, 128).transpose(2, 1, 0, 3))
    com["w_out_t"] = np.ascontiguousarray(
        w_out.reshape(16, 128, 8, 128).transpose(2, 1, 0, 3))
    com["w_gx_t"] = np.ascontiguousarray(
        w_gx.reshape(8, 2, 128, 128).transpose(2, 0, 1, 3))
    com["w_ga_t"] = np.ascontiguousarray(
        w_ga.reshape(8, 2, 128, 128).transpose(2, 0, 1, 3))
    com["w1_t"] = np.ascontiguousarray(
        w1.reshape(8, 8, 128, 22, 128).transpose(0, 3, 2, 1, 4))
    com["w2_t"] = np.ascontiguousarray(w2.reshape(8, 11, 128, 1024))
    com["rw_t"] = np.ascontiguousarray(rw.reshape(8, 128, 8).transpose(1, 0, 2))
    com["cin"] = np.ascontiguousarray(np.concatenate(
        [_vec_tile(arp, 8), _vec_tile(aip, 8),
         _vec_tile(b_gx, 8), _vec_tile(b_ga, 8)], axis=1))
    com["binv"] = _vec_tile(b_in, 16)
    com["boutv"] = _vec_tile(b_out, 8)
    com["identf"] = np.eye(128, dtype=np.float32)
    com["identb"] = np.eye(128, dtype=ml_dtypes.bfloat16)
    com["iota"] = np.arange(128, dtype=np.float32).reshape(128, 1)
    com["bigc"] = np.full((NE, PCAP, 2), 1e6, dtype=np.float32)
    return x, com



def _emit(E, T):
    x_d = T["x"]
    out_d = T["out"]

    pl = T["pl"]
    PS = T["ps"]
    cvec, cin, binv, lnt, carry = T["cvec"], T["cin"], T["binv"], T["lnt"], T["carry"]
    xng, gmoe = T["xng"], T["gmoe"]
    wgates, rw_sb, win_sb, wout_sb = T["wgates"], T["rw_sb"], T["win_sb"], T["wout_sb"]
    y1t, y2t, silu_sb = T["y1t"], T["y2t"], T["silu_sb"]
    identf, identb = T["identf"], T["identb"]
    rlg, ltok, lp, msk, rtm, mfm, posf, rcount, idxf, idxi, tokv, iota_sb = (
        T["rlg"], T["ltok"], T["lp"], T["msk"], T["rtm"], T["mfm"],
        T["posf"], T["rcount"], T["idxf"], T["idxi"], T["tokv"], T["iota_sb"])
    xef, xetm, w2rt, o_tok, pslot, sidxi, yfm, otm = (
        T["xef"], T["xetm"], T["w2rt"], T["o_tok"], T["pslot"],
        T["sidxi"], T["yfm"], T["otm"])

    for nm, dst, src in [("c_identf", identf, T["identf_d"]),
                         ("c_identb", identb, T["identb_d"]),
                         ("c_iota", iota_sb, T["iota_d"]),
                         ("c_cin", cin, T["cin_d"]),
                         ("c_binv", binv, T["binv_d"])]:
        E.dma("s", lambda s, d=dst, sr=src: s.dma_start(out=d[:], in_=sr))
        E.dmark("s", nm)
    E.dma("s", lambda s: s.dma_start(out=cvec[:, 40:48], in_=T["boutv_d"]))
    E.dmark("s", "c_bout")

    E.dma("g", lambda g: g.dma_start(
        out=wgates[:, 0:2048],
        in_=T["w_gx_d"].rearrange("p n k c -> p (n k c)")))
    E.dmark("g", "c_wgx")
    E.dma("g", lambda g: g.dma_start(
        out=wgates[:, 2048:4096],
        in_=T["w_ga_d"].rearrange("p n k c -> p (n k c)")))
    E.dmark("g", "c_wga")
    E.dma("g", lambda g: g.dma_start(
        out=rw_sb[:], in_=T["rw_d"].rearrange("p k j -> p (k j)")))
    E.dmark("g", "c_rw")
    E.dma("g", lambda g: g.dma_start(out=T["wibf"].ap(), in_=T["w_in_d"]))
    E.dmark("g", "wibf")
    E.dma("g", lambda g: g.dma_start(out=T["wobf"].ap(), in_=T["w_out_d"]))
    E.dmark("g", "wobf")
    for e in range(NE):
        E.dma("g", lambda g, e=e: g.dma_start(
            out=T["w1bf"].ap()[e], in_=T["w1_d"][e]))
        E.dmark("g", ("cast_w1", e))
        E.dma("g", lambda g, e=e: g.dma_start(
            out=T["w2bf"].ap()[e], in_=T["w2_d"][e]))
        E.dmark("g", ("cast_w2", e))

    E.wait("a", "c_cin")
    E.opsync("a", lambda a: a.activation(out=cvec[:, 48:56], in_=cin[:, 0:8],
                                         func=AF.Abs))
    E.opsync("a", lambda a: a.activation(out=cvec[:, 56:64], in_=cvec[:, 48:56],
                                         func=AF.Exp, scale=-1.0))
    E.op("a", lambda a: a.activation(out=cvec[:, 48:56], in_=cvec[:, 56:64],
                                     func=AF.Ln, bias=1.0))
    E.mark("a", "sp_ln")
    E.wait("v", "c_cin")
    E.wait("v", "c_bout")
    E.wait("v", "sp_ln")
    E.opsync("v", lambda v: v.tensor_scalar(out=cvec[:, 56:64], in0=cin[:, 0:8],
                                            scalar1=0.0, scalar2=None, op0=ALU.max))
    E.opsync("v", lambda v: v.tensor_tensor(out=cvec[:, 48:56], in0=cvec[:, 48:56],
                                            in1=cvec[:, 56:64], op=ALU.add))
    E.opsync("v", lambda v: v.tensor_scalar(out=cvec[:, 0:8], in0=cvec[:, 48:56],
                                            scalar1=-4.0, scalar2=None, op0=ALU.mult))
    E.opsync("v", lambda v: v.tensor_scalar(out=cvec[:, 8:16], in0=cvec[:, 48:56],
                                            scalar1=-8.0, scalar2=None, op0=ALU.mult))
    E.opsync("v", lambda v: v.tensor_scalar(out=cvec[:, 16:24], in0=cin[:, 8:16],
                                            scalar1=INV4PI, scalar2=None, op0=ALU.mult))
    E.opsync("v", lambda v: v.tensor_scalar(out=cvec[:, 24:32], in0=cin[:, 16:24],
                                            scalar1=0.5, scalar2=None, op0=ALU.mult))
    E.opsync("v", lambda v: v.tensor_scalar(out=cvec[:, 32:40], in0=cin[:, 24:32],
                                            scalar1=0.5, scalar2=None, op0=ALU.mult))
    E.mark("v", "cvec")

    REG = {}

    def _mkregs(gg):
        r1 = gg.alloc_register("cap_bound")
        gg.reg_mov(r1, PCAP - 1)
        r2 = gg.alloc_register("tok_bound")
        gg.reg_mov(r2, S - 1)
        REG["cap"] = r1
        REG["tok"] = r2
        return gg.engine_nop()

    if not SKIP_MOE:
        E.op("g", _mkregs)

    bank_use = {k: 0 for k in PS}
    bank_rel = {k: 0 for k in PS}

    def bank_acquire(bank, st):
        n = bank_use[bank]
        bank_use[bank] = n + 1
        if n > 0:
            E.wait(st, ("bk", bank, n - 1))
        return n

    def bank_release(bank, src_label):
        n = bank_rel[bank]
        bank_rel[bank] = n + 1
        E.alias(("bk", bank, n), src_label)

    def widx_of(g, i):
        return g * NW + i

    def sync_slotA(g, i):
        if (i == -1 or i % 2 == 1) and 0 <= i + 1 < NW:
            pb = widx_of(g, i + 1) // 2
            for m in range(NMH):
                u = pb * NMH + m
                slot = u % 3
                if u >= 3:
                    pu = u - 3
                    E.wait("s", ("xh_ps", pu // NMH, pu % NMH))
                else:
                    E.wait("s", "wibf")
                E.dma("s", lambda s, m=m, slot=slot: s.dma_start(
                    out=win_sb[:, slot * 1024:(slot + 1) * 1024],
                    in_=T["wibf"].ap()[m].rearrange("p k c -> p (k c)")))
                E.dmark("s", ("winL", u))
    def sync_woL(g, i):
        if 0 <= i < NW:
            wx = widx_of(g, i)
            for u2 in range(2 * NQ):
                m, h = u2 // 2, u2 % 2
                u = wx * 2 * NQ + u2
                slot = u % 4
                if u >= 4:
                    pu = u - 4
                    E.wait("s", ("wo_ps", pu // (2 * NQ), pu % (2 * NQ)))
                else:
                    E.wait("s", "wobf")
                E.dma("s", lambda s, m=m, h=h, slot=slot: s.dma_start(
                    out=wout_sb[:, slot * 1024:(slot + 1) * 1024],
                    in_=T["wobf"].ap()[m, :, 8 * h:8 * (h + 1), :].rearrange(
                        "p k c -> p (k c)")))
                E.dmark("s", ("woL", u))

    def sync_loadx(g, i):
        wx = widx_of(g, i)
        par = wx % 3
        seq = g // 4
        t0 = (g % 4) * TG + i * W
        if wx >= 3:
            E.wait("s", ("osum", wx - 3))
        E.dma("s", lambda s: s.dma_start(
            out=pl["xtm"][par][:], in_=x_d[seq, t0:t0 + W, :]))
        E.dmark("s", ("xL", wx))

    def pe_router(g, i):
        wx = widx_of(g, i)
        if SKIP_MOE:
            return
        E.wait("t", "c_rw")
        bank_acquire("pR", "t")
        for k in range(NQ):
            E.op("t", lambda t, k=k: t.matmul(
                out=PS["pR"][0:8, 0:W],
                lhsT=rw_sb[:, k * 8:(k + 1) * 8],
                rhs=xng[:, (k * 2 + i % 2) * W:
                        (k * 2 + i % 2 + 1) * W],
                start=(k == 0), stop=(k == NQ - 1)))
        E.mark("t", ("lg_ps", wx))
        E.wait("t", "c_identf")
        E.wait("t", ("lg_ev", wx))
        bank_acquire("pR", "t")
        E.op("t", lambda t: t.transpose(
            out=PS["pR"][:, 0:8], in_=rlg[0:8, 0:W],
            identity=identf[0:8, 0:8]))
        E.mark("t", ("ltr", wx))

    def pe_slotA(g, i):
        wx = widx_of(g, i)
        if 0 <= i < NW and i % 2 == 0:
            E.wait("t", "c_identb")
            for w2 in range(2):
                ww = wx + w2
                E.wait("t", ("xn_tm", ww))
                bank_acquire("pT", "t")
                for q in range(NQ):
                    E.op("t", lambda t, q=q, pp=ww % 2: t.transpose(
                        out=PS["pT"][:, q * 128:(q + 1) * 128],
                        in_=pl["xntm"][pp][:, q * 128:(q + 1) * 128],
                        identity=identb[:]))
                E.mark("t", ("xnT", ww))
            E.wait("t", ("xn_tev", wx))
            pe_router(g, i)
            E.wait("t", ("xn_tev", wx + 1))
            pb = wx // 2
            for m in range(NMH):
                u = pb * NMH + m
                bank = "pA" if m % 2 == 0 else "pB"
                E.wait("t", ("winL", u))
                bank_acquire(bank, "t")
                slot = u % 3
                for w2 in range(2):
                    for k in range(NQ):
                        E.op("t", lambda t, k=k, slot=slot, w2=w2, bank=bank:
                             t.matmul(
                                 out=PS[bank][:, w2 * W:(w2 + 1) * W],
                                 lhsT=win_sb[:, (slot * 8 + k) * 128:
                                             (slot * 8 + k + 1) * 128],
                                 rhs=xng[:, (k * 2 + (i + w2) % 2) * W:
                                         (k * 2 + (i + w2) % 2 + 1) * W],
                                 start=(k == 0), stop=(k == NQ - 1)))
                E.mark("t", ("xh_ps", pb, m))
        if 0 <= i < NW:
            par = wx % 2
            E.wait("t", "c_wgx")
            E.wait("t", "c_wga")
            for gate in range(2):
                for n_ in range(NQ):
                    bank = "pC" if n_ % 2 == 0 else "pD"
                    E.wait("t", ("xh_ev", wx, 2 * n_ + 1))
                    bank_acquire(bank, "t")
                    for k in range(2):
                        tile = gate * 16 + n_ * 2 + k
                        E.op("t", lambda t, tile=tile, n_=n_, gate=gate, k=k: t.matmul(
                            out=PS[("pC" if n_ % 2 == 0 else "pD")][:, 0:W],
                            lhsT=wgates[:, tile * 128:(tile + 1) * 128],
                            rhs=pl["xh"][wx % 3][:, (2 * n_ + k) * 128:(2 * n_ + k + 1) * 128],
                            start=(k == 0), stop=(k == 1)))
                    E.mark("t", ("g_ps", wx, gate, n_))
            if i % 2 == 1:
                pe_router(g, i)
        if 1 <= i <= NW:
            pwx = widx_of(g, i - 1)
            E.wait("t", ("hm", pwx))
            for m in range(NQ):
                bank = "pE" if m % 2 == 0 else "pF"
                bank_acquire(bank, "t")
                for h in range(2):
                    u = pwx * 2 * NQ + m * 2 + h
                    E.wait("t", ("woL", u))
                    slot = u % 4
                    for k in range(8 * h, 8 * h + 8):
                        E.op("t", lambda t, k=k, h=h, slot=slot, bank=bank:
                             t.matmul(
                                 out=PS[bank][:, 0:W],
                                 lhsT=wout_sb[:, (slot * 8 + k - 8 * h) * 128:
                                              (slot * 8 + k - 8 * h + 1) * 128],
                                 rhs=pl["hm"][:, k * 128:(k + 1) * 128],
                                 start=(k == 0), stop=(k == NMH - 1)))
                    E.mark("t", ("wo_ps", pwx, m * 2 + h))
        if 0 <= i < NW and not SKIP_MOE:
            E.wait("t", ("msk_rdy", wx))
            bank_acquire("pR", "t")
            E.op("t", lambda t: t.transpose(
                out=PS["pR"][0:8, 0:128], in_=msk[:, 0:8],
                identity=identf[:]))
            E.mark("t", ("mtr", wx))
            E.wait("t", ("pos_rdy", wx))
            bank_acquire("pR", "t")
            E.op("t", lambda t: t.transpose(
                out=PS["pR"][:, 0:8], in_=posf[0:8, 0:W],
                identity=identf[0:8, 0:8]))
            E.mark("t", ("ptr2", wx))
        if 1 <= i <= NW:
            pwx = widx_of(g, i - 1)
            E.wait("t", ("yfm", pwx))
            for half in range(2):
                ob = "pE" if half == 0 else "pF"
                bank_acquire(ob, "t")
                for q in range(4):
                    qq = half * 4 + q
                    E.op("t", lambda t, q=q, qq=qq, ob=ob: t.transpose(
                        out=PS[ob][:, q * 128:(q + 1) * 128],
                        in_=yfm[:, qq * 128:(qq + 1) * 128],
                        identity=identf[:]))
                E.mark("t", ("ot_ps", pwx, half))
    def act_slotA(g, i):
        wx = widx_of(g, i)
        if 1 <= i <= NW:
            pwx = widx_of(g, i - 1)
            if pwx >= 1:
                E.wait("a", ("hm", pwx - 1))
            E.wait("a", ("fr", pwx))
            E.op("a", lambda a: a.activation(out=pl["sn"][:], in_=pl["p1"][:],
                                             func=AF.Sin, scale=TWO_PI))
            E.mark("a", ("sn", pwx))
            E.wait("a", ("fr2", pwx))
            E.op("a", lambda a: a.activation(out=pl["cs"][:], in_=pl["p2"][:],
                                             func=AF.Sin, scale=TWO_PI))
            E.mark("a", ("cs", pwx))
        if 0 <= i + 1 < NW:
            nwx = widx_of(g, i + 1)
            p1_ = nwx % 2
            E.wait("a", ("var", nwx))
            E.opsync("a", lambda a, p1_=p1_: a.activation(
                out=lnt[:, 10:11], in_=lnt[:, 6 + 3 * p1_:7 + 3 * p1_],
                func=AF.Ln))
            E.op("a", lambda a, p1_=p1_: a.activation(
                out=lnt[:, 2 * p1_ + 1:2 * p1_ + 2], in_=lnt[:, 10:11],
                func=AF.Exp, scale=-0.5))
            E.mark("a", ("rstd", nwx))
        if 0 <= i < NW and i % 2 == 0:
            for w2 in range(2):
                ww = wx + w2
                E.wait("a", ("xnT", ww))
                E.op("a", lambda a, iw=(i + w2) % 2: a.copy(
                    out=xng[:].rearrange("p (q t) -> p q t", q=NQ)[
                        :, :, iw * W:(iw + 1) * W],
                    in_=PS["pT"][:, 0:1024]))
                E.mark("a", ("xn_tev", ww))
                bank_release("pT", ("xn_tev", ww))
            if not SKIP_MOE:
                E.wait("a", ("lg_ps", wx))
                E.op("a", lambda a: a.copy(out=rlg[0:8, 0:W],
                                           in_=PS["pR"][0:8, 0:W]))
                E.mark("a", ("lg_ev", wx))
                bank_release("pR", ("lg_ev", wx))
            E.wait("a", "cvec")
            E.wait("a", "c_binv")
            pb = wx // 2
            for m in range(NMH):
                bank = "pA" if m % 2 == 0 else "pB"
                E.wait("a", ("xh_ps", pb, m))
                for w2 in range(2):
                    ww = wx + w2
                    if m == 0 and ww >= 3:
                        E.wait("a", ("hm", ww - 3))
                    E.op("a", lambda a, m=m, w2=w2, bank=bank, r3=(wx + w2) % 3:
                         a.activation(
                             out=pl["xh"][r3][:, m * 128:(m + 1) * 128],
                             in_=PS[bank][:, w2 * W:(w2 + 1) * W],
                             func=AF.Identity, bias=binv[:, m:m + 1]))
                    E.mark("a", ("xh_ev", ww, m))
                bank_release(bank, ("xh_ev", wx + 1, m))
        if 0 <= i < NW:
            par = wx % 2
            for gate in range(2):
                dst = pl["tx"][par] if gate == 0 else pl["ta"]
                bias0 = 24 if gate == 0 else 32
                for n_ in range(NQ):
                    E.wait("a", ("g_ps", wx, gate, n_))
                    E.op("a", lambda a, dst=dst, n_=n_, bias0=bias0: a.activation(
                        out=dst[:, n_ * 128:(n_ + 1) * 128],
                        in_=PS[("pC" if n_ % 2 == 0 else "pD")][:, 0:W],
                        func=AF.Tanh, scale=0.5, bias=cvec[:, bias0 + n_:bias0 + n_ + 1]))
                    E.mark("a", ("g_ev", wx, gate, n_))
                    bank_release("pC" if n_ % 2 == 0 else "pD", ("g_ev", wx, gate, n_))
            if not SKIP_MOE and i % 2 == 1:
                E.wait("a", ("lg_ps", wx))
                E.op("a", lambda a: a.copy(out=rlg[0:8, 0:W], in_=PS["pR"][0:8, 0:W]))
                E.mark("a", ("lg_ev", wx))
                bank_release("pR", ("lg_ev", wx))
        if 0 <= i < NW and not SKIP_MOE:
            rpar = wx % 2
            E.wait("a", ("rsub", wx))
            E.op("a", lambda a, rpar=rpar: a.activation(
                out=lp[rpar][:], in_=ltok[:], func=AF.Exp))
            E.mark("a", ("rexp", wx))
        if 0 <= i < NW:
            par = wx % 2
            for q in range(NQ):
                E.op("a", lambda a, q=q, par=par: a.activation(
                    out=pl["rho"][par][:, q * 128:(q + 1) * 128],
                    in_=pl["ta"][:, q * 128:(q + 1) * 128],
                    func=AF.Exp, scale=cvec[:, q:q + 1], bias=cvec[:, q:q + 1]))
            E.mark("a", ("rho", wx))
            for q in range(NQ):
                E.op("a", lambda a, q=q: a.activation(
                    out=pl["pv"][:, q * 128:(q + 1) * 128],
                    in_=pl["ta"][:, q * 128:(q + 1) * 128],
                    func=AF.Exp, scale=cvec[:, 8 + q:8 + q + 1],
                    bias=cvec[:, 8 + q:8 + q + 1]))
            E.mark("a", ("rho2", wx))
        if 1 <= i <= NW:
            pwx = widx_of(g, i - 1)
            for m in range(NQ):
                E.wait("a", ("wo_ps", pwx, 2 * m + 1))
                E.op("a", lambda a, m=m: a.activation(
                    out=yfm[:, m * 128:(m + 1) * 128],
                    in_=PS[("pE" if m % 2 == 0 else "pF")][:, 0:W],
                    func=AF.Identity, bias=cvec[:, 40 + m:40 + m + 1]))
                E.mark("a", ("wo_ev", pwx, m))
                bank_release("pE" if m % 2 == 0 else "pF", ("wo_ev", pwx, m))
            E.alias(("yfm", pwx), ("wo_ev", pwx, NQ - 1))
            opar = pwx % 2
            for half in range(2):
                ob = "pE" if half == 0 else "pF"
                E.wait("a", ("ot_ps", pwx, half))
                if pwx >= 2:
                    E.wait("a", ("outw", pwx - 2))
                E.op("a", lambda a, half=half, opar=opar, ob=ob: a.copy(
                    out=otm[opar][:, half * 512:(half + 1) * 512],
                    in_=PS[ob][:, 0:512]))
                E.mark("a", ("ot_ev", pwx, half))
                bank_release(ob, ("ot_ev", pwx, half))

    def act_nrm(g, i):
        wx = widx_of(g, i)
        if 0 <= i < NW:
            par = wx % 2
            E.wait("a", ("v", wx))
            E.op("a", lambda a, par=par: a.activation(out=pl["pnrm"][par][:],
                                                      in_=pl["pv"][:], func=AF.Ln))
            E.op("a", lambda a, par=par: a.activation(out=pl["pnrm"][par][:],
                                                      in_=pl["pnrm"][par][:],
                                                      func=AF.Exp, scale=0.5))
            E.mark("a", ("nrm", wx))

    def dve_slotA(g, i):
        wx = widx_of(g, i)
        if 0 <= i + 1 < NW:
            nwx = widx_of(g, i + 1)
            p1_ = nwx % 2
            px_ = nwx % 3
            E.wait("v", ("xL", nwx))
            E.op("v", lambda v, px_=px_: v.tensor_tensor(
                out=pl["psq"][:], in0=pl["xtm"][px_][:], in1=pl["xtm"][px_][:],
                op=ALU.mult))
            E.opsync("v", lambda v, p1_=p1_, px_=px_: v.reduce_sum(
                out=lnt[:, 4 + 3 * p1_:5 + 3 * p1_], in_=pl["xtm"][px_][:], axis=AX))
            E.opsync("v", lambda v, p1_=p1_: v.reduce_sum(
                out=lnt[:, 5 + 3 * p1_:6 + 3 * p1_], in_=pl["psq"][:], axis=AX))
            E.opsync("v", lambda v, p1_=p1_: v.tensor_scalar(
                out=lnt[:, 2 * p1_:2 * p1_ + 1], in0=lnt[:, 4 + 3 * p1_:5 + 3 * p1_],
                scalar1=1.0 / D, scalar2=None, op0=ALU.mult))
            E.opsync("v", lambda v, p1_=p1_: v.tensor_scalar(
                out=lnt[:, 5 + 3 * p1_:6 + 3 * p1_], in0=lnt[:, 5 + 3 * p1_:6 + 3 * p1_],
                scalar1=1.0 / D, scalar2=None, op0=ALU.mult))
            E.opsync("v", lambda v, p1_=p1_: v.tensor_tensor(
                out=lnt[:, 11:12], in0=lnt[:, 2 * p1_:2 * p1_ + 1],
                in1=lnt[:, 2 * p1_:2 * p1_ + 1], op=ALU.mult))
            E.opsync("v", lambda v, p1_=p1_: v.tensor_tensor(
                out=lnt[:, 6 + 3 * p1_:7 + 3 * p1_], in0=lnt[:, 5 + 3 * p1_:6 + 3 * p1_],
                in1=lnt[:, 11:12], op=ALU.subtract))
            E.opsync("v", lambda v, p1_=p1_: v.tensor_scalar(
                out=lnt[:, 6 + 3 * p1_:7 + 3 * p1_], in0=lnt[:, 6 + 3 * p1_:7 + 3 * p1_],
                scalar1=1e-5, scalar2=None, op0=ALU.add))
            E.mark("v", ("var", nwx))
            E.wait("v", ("rstd", nwx))
            if not SKIP_MOE and nwx >= 2:
                E.wait("v", ("scat", nwx - 2))
            E.op("v", lambda v, p1_=p1_, px_=px_: v.tensor_scalar(
                out=pl["xntm"][p1_][:], in0=pl["xtm"][px_][:],
                scalar1=lnt[:, 2 * p1_:2 * p1_ + 1],
                scalar2=lnt[:, 2 * p1_ + 1:2 * p1_ + 2],
                op0=ALU.subtract, op1=ALU.mult))
            E.mark("v", ("xn_tm", nwx))
        if 1 <= i <= NW:
            pwx = widx_of(g, i - 1)
            par = pwx % 2
            first = (pwx % (4 * NW) == 0)
            E.wait("v", ("nrm", pwx))
            E.wait("v", ("cs", pwx))
            E.wait("v", ("sn", pwx))
            E.wait("v", ("xh_ev", pwx, NMH - 1))
            E.op("v", lambda v, par=par: v.tensor_scalar(
                out=pl["tx"][par][:], in0=pl["tx"][par][:],
                scalar1=0.5, scalar2=0.5, op0=ALU.mult, op1=ALU.add))
            E.op("v", lambda v, par=par: v.tensor_tensor(
                out=pl["tx"][par][:], in0=pl["tx"][par][:],
                in1=pl["pnrm"][par][:], op=ALU.mult))
            E.op("v", lambda v, par=par, r3=pwx % 3: v.tensor_tensor(
                out=pl["hr"][:], in0=pl["tx"][par][:],
                in1=pl["xh"][r3][:, 0:1024], op=ALU.mult))
            E.op("v", lambda v, par=par, r3=pwx % 3: v.tensor_tensor(
                out=pl["hi"][:], in0=pl["tx"][par][:],
                in1=pl["xh"][r3][:, 1024:2048], op=ALU.mult))
            E.op("v", lambda v: v.tensor_tensor(out=pl["p1"][:], in0=pl["hr"][:],
                                                in1=pl["sn"][:], op=ALU.mult))
            E.op("v", lambda v: v.tensor_tensor(out=pl["hr"][:], in0=pl["hr"][:],
                                                in1=pl["cs"][:], op=ALU.mult))
            E.op("v", lambda v: v.tensor_tensor(out=pl["p2"][:], in0=pl["hi"][:],
                                                in1=pl["sn"][:], op=ALU.mult))
            E.op("v", lambda v: v.tensor_tensor(out=pl["hi"][:], in0=pl["hi"][:],
                                                in1=pl["cs"][:], op=ALU.mult))
            E.op("v", lambda v: v.tensor_tensor(out=pl["hr"][:], in0=pl["hr"][:],
                                                in1=pl["p2"][:], op=ALU.add))
            E.op("v", lambda v: v.tensor_tensor(out=pl["hi"][:], in0=pl["hi"][:],
                                                in1=pl["p1"][:], op=ALU.subtract))
            if pwx >= 1:
                E.wait("v", ("wo_ps", pwx - 1, 2 * NQ - 1))
            if pwx >= 1:
                E.wait("v", ("hcar", pwx - 1))
            E.wait("v", ("rho", pwx))
            for q in range(NQ):
                init = 0.0 if first else carry[:, 8 + q:8 + q + 1]
                E.op("v", lambda v, q=q, init=init, par=par: v.tensor_tensor_scan(
                    out=pl["hr"][:, q * 128:(q + 1) * 128],
                    data0=pl["rho"][par][:, q * 128:(q + 1) * 128],
                    data1=pl["hr"][:, q * 128:(q + 1) * 128],
                    initial=init, op0=ALU.mult, op1=ALU.add))
            for q in range(NQ):
                init = 0.0 if first else carry[:, 16 + q:16 + q + 1]
                E.op("v", lambda v, q=q, init=init, par=par: v.tensor_tensor_scan(
                    out=pl["hi"][:, q * 128:(q + 1) * 128],
                    data0=pl["rho"][par][:, q * 128:(q + 1) * 128],
                    data1=pl["hi"][:, q * 128:(q + 1) * 128],
                    initial=init, op0=ALU.mult, op1=ALU.add))
            E.opsync("v", lambda v: v.engine_nop())
            E.opsync("v", lambda v: v.tensor_copy(out=carry[:, 8:16],
                                                  in_=pl["hr"][:, 127::128]))
            E.opsync("v", lambda v: v.tensor_copy(out=carry[:, 16:24],
                                                  in_=pl["hi"][:, 127::128]))
            E.mark("v", ("hcar", pwx))
            E.op("v", lambda v: v.tensor_tensor(out=pl["p1"][:], in0=pl["hr"][:],
                                                in1=pl["sn"][:], op=ALU.mult))
            E.op("v", lambda v: v.tensor_tensor(out=pl["p2"][:], in0=pl["hi"][:],
                                                in1=pl["sn"][:], op=ALU.mult))
            E.op("v", lambda v: v.tensor_tensor(out=pl["hr"][:], in0=pl["hr"][:],
                                                in1=pl["cs"][:], op=ALU.mult))
            E.op("v", lambda v: v.tensor_tensor(out=pl["hi"][:], in0=pl["hi"][:],
                                                in1=pl["cs"][:], op=ALU.mult))
            E.op("v", lambda v: v.tensor_tensor(
                out=pl["hm"][:, 0:1024], in0=pl["hr"][:], in1=pl["p2"][:],
                op=ALU.subtract))
            E.op("v", lambda v: v.tensor_tensor(
                out=pl["hm"][:, 1024:2048], in0=pl["hi"][:], in1=pl["p1"][:],
                op=ALU.add))
            E.mark("v", ("hm", pwx))
        if 0 <= i < NW and not SKIP_MOE:
            rpar = wx % 2
            E.wait("v", ("ltr", wx))
            E.opsync("v", lambda v: v.tensor_copy(out=ltok[:], in_=PS["pR"][:, 0:8]))
            bank_release("pR", ("ss", E._ss))
            E.opsync("v", lambda v: v.reduce_max(out=rtm[:, 0:1], in_=ltok[:], axis=AX))
            E.op("v", lambda v: v.tensor_scalar(
                out=ltok[:], in0=ltok[:], scalar1=rtm[:, 0:1], scalar2=None,
                op0=ALU.subtract))
            E.mark("v", ("rsub", wx))
            E.wait("v", ("rexp", wx))
            E.opsync("v", lambda v, rpar=rpar: v.reduce_sum(
                out=rtm[:, 1:2], in_=lp[rpar][:], axis=AX))
            E.opsync("v", lambda v: v.reciprocal(out=rtm[:, 2:3], in_=rtm[:, 1:2]))
            E.opsync("v", lambda v, rpar=rpar: v.tensor_scalar(
                out=lp[rpar][:], in0=lp[rpar][:], scalar1=rtm[:, 2:3], scalar2=None,
                op0=ALU.mult))
            E.opsync("v", lambda v, rpar=rpar: v.tensor_scalar(
                out=msk[:, 0:8], in0=lp[rpar][:], scalar1=rtm[:, 2:3], scalar2=None,
                op0=ALU.is_ge))
            E.opsync("v", lambda v, rpar=rpar: v.scalar_tensor_tensor(
                out=ltok[:], in0=msk[:, 0:8], scalar=-4.0, in1=lp[rpar][:],
                op0=ALU.mult, op1=ALU.add))
            E.opsync("v", lambda v: v.reduce_max(out=rtm[:, 3:4], in_=ltok[:], axis=AX))
            E.opsync("v", lambda v, rpar=rpar: v.tensor_scalar(
                out=msk[:, 0:8], in0=lp[rpar][:], scalar1=rtm[:, 3:4], scalar2=None,
                op0=ALU.is_ge))
            E.opsync("v", lambda v, rpar=rpar: v.tensor_tensor(
                out=lp[rpar][:], in0=lp[rpar][:], in1=msk[:, 0:8], op=ALU.mult))
            E.mark("v", ("msk_rdy", wx))
            E.wait("v", ("mtr", wx))
            E.opsync("v", lambda v: v.tensor_copy(out=mfm[0:8, 0:W],
                                                  in_=PS["pR"][0:8, 0:128]))
            bank_release("pR", ("ss", E._ss))
            first_w = (i == 0 and g % 2 == 0)
            if not first_w:
                E.wait("v", ("rcnt", wx - 1))
            if first_w:
                E.opsync("v", lambda v: v.tensor_scalar(
                    out=rcount[0:8, 0:1], in0=iota_sb[0:8, 0:1], scalar1=0.0,
                    scalar2=None, op0=ALU.is_lt))
            E.opsync("v", lambda v: v.tensor_tensor_scan(
                out=posf[0:8, 0:W], data0=mfm[0:8, 0:W], data1=mfm[0:8, 0:W],
                initial=rcount[0:8, 0:1], op0=ALU.add, op1=ALU.bypass))
            E.opsync("v", lambda v: v.tensor_copy(out=rcount[0:8, 0:1],
                                                  in_=posf[0:8, W - 1:W]))
            E.mark("v", ("rcnt", wx))
            E.opsync("v", lambda v: v.tensor_scalar(
                out=posf[0:8, 0:W], in0=posf[0:8, 0:W], scalar1=1.0, scalar2=None,
                op0=ALU.subtract))
            E.opsync("v", lambda v: v.scalar_tensor_tensor(
                out=posf[0:8, 0:W], in0=mfm[0:8, 0:W], scalar=-1e6,
                in1=posf[0:8, 0:W], op0=ALU.mult, op1=ALU.add))
            E.opsync("v", lambda v: v.tensor_scalar(
                out=posf[0:8, 0:W], in0=posf[0:8, 0:W], scalar1=1e6, scalar2=None,
                op0=ALU.add))
            E.mark("v", ("pos_rdy", wx))
            E.wait("v", ("ptr2", wx))
            E.opsync("v", lambda v: v.tensor_copy(out=idxf[:], in_=PS["pR"][:, 0:8]))
            bank_release("pR", ("ss", E._ss))
            E.opsync("v", lambda v, rpar=rpar: v.tensor_copy(
                out=idxi[rpar][:], in_=idxf[:]))
            tbase = float((g % 4) * TG + i * W)
            E.opsync("v", lambda v, rpar=rpar, tbase=tbase: v.tensor_scalar(
                out=tokv[rpar][:], in0=iota_sb[:], scalar1=tbase, scalar2=None,
                op0=ALU.add))
            for e in range(NE):
                E.op("v", lambda v, e=e, rpar=rpar: v.tensor_copy(
                    out=T["lpt"][rpar][:, 2 * e:2 * e + 1], in_=lp[rpar][:, e:e + 1]))
                E.op("v", lambda v, e=e, rpar=rpar: v.tensor_copy(
                    out=T["lpt"][rpar][:, 2 * e + 1:2 * e + 2], in_=tokv[rpar][:]))
            E.opsync("v", lambda v: v.engine_nop())
            E.mark("v", ("idx", wx))
        if 0 <= i < NW:
            par = wx % 2
            first = (wx % (4 * NW) == 0)
            E.wait("v", ("g_ev", wx, 1, NQ - 1))
            E.wait("v", "cvec")
            if i >= 1 or g >= 1:
                pw2 = wx - 1
                E.wait("v", ("cs", pw2))
            for q in range(NQ):
                E.op("v", lambda v, q=q: v.tensor_scalar(
                    out=pl["p1"][:, q * 128:(q + 1) * 128],
                    in0=pl["ta"][:, q * 128:(q + 1) * 128],
                    scalar1=cvec[:, 16 + q:16 + q + 1],
                    scalar2=cvec[:, 16 + q:16 + q + 1],
                    op0=ALU.mult, op1=ALU.add))
            if wx >= 1:
                E.wait("v", ("phic", wx - 1))
            for q in range(NQ):
                init = 0.0 if first else carry[:, q:q + 1]
                E.op("v", lambda v, q=q, init=init: v.tensor_tensor_scan(
                    out=pl["p2"][:, q * 128:(q + 1) * 128],
                    data0=pl["p1"][:, q * 128:(q + 1) * 128],
                    data1=pl["p1"][:, q * 128:(q + 1) * 128],
                    initial=init, op0=ALU.add, op1=ALU.bypass))
            E.opsync("v", lambda v: v.engine_nop())
            E.opsync("v", lambda v: v.tensor_copy(out=carry[:, 0:8],
                                                  in_=pl["p2"][:, 127::128]))
            E.mark("v", ("phic", wx))
            def range_reduce(dst):
                for h in range(2):
                    E.op("v", lambda v, h=h, dst=dst: v.tensor_copy(
                        out=pl["pi"][:], in_=pl[dst][:, h * 512:(h + 1) * 512]))
                    E.op("v", lambda v, h=h: v.tensor_copy(
                        out=pl["hr"][:, h * 512:(h + 1) * 512], in_=pl["pi"][:]))
                E.op("v", lambda v, dst=dst: v.scalar_tensor_tensor(
                    out=pl[dst][:], in0=pl["hr"][:], scalar=-1.0, in1=pl[dst][:],
                    op0=ALU.mult, op1=ALU.add))
                E.op("v", lambda v, dst=dst: v.tensor_scalar(
                    out=pl["hr"][:], in0=pl[dst][:], scalar1=0.0, scalar2=None,
                    op0=ALU.is_lt))
                E.op("v", lambda v, dst=dst: v.tensor_scalar(
                    out=pl[dst][:], in0=pl[dst][:], scalar1=0.5, scalar2=None,
                    op0=ALU.subtract))
                E.op("v", lambda v, dst=dst: v.tensor_tensor(
                    out=pl[dst][:], in0=pl[dst][:], in1=pl["hr"][:], op=ALU.add))
            E.op("v", lambda v: v.tensor_copy(out=pl["p1"][:], in_=pl["p2"][:]))
            range_reduce("p1")
            E.mark("v", ("fr", wx))
            E.op("v", lambda v: v.tensor_scalar(out=pl["p2"][:], in0=pl["p2"][:],
                                                scalar1=0.25, scalar2=None,
                                                op0=ALU.add))
            range_reduce("p2")
            E.mark("v", ("fr2", wx))
            E.wait("v", ("rho2", wx))
            E.op("v", lambda v: v.tensor_scalar(out=pl["pv"][:], in0=pl["pv"][:],
                                                scalar1=-1.0, scalar2=1.0,
                                                op0=ALU.mult, op1=ALU.add))
            E.op("v", lambda v: v.tensor_scalar(out=pl["pv"][:], in0=pl["pv"][:],
                                                scalar1=1e-6, scalar2=None,
                                                op0=ALU.max))
            E.mark("v", ("v", wx))
        if 1 <= i <= NW:
            pwx = widx_of(g, i - 1)
            opar = pwx % 2
            pxr = pwx % 3
            E.wait("v", ("ot_ev", pwx, 1))
            E.op("v", lambda v, opar=opar, pxr=pxr: v.tensor_tensor(
                out=otm[opar][:], in0=otm[opar][:],
                in1=pl["xtm"][pxr][:], op=ALU.add))
            E.mark("v", ("osum", pwx))

    def g_slotA(g, i):
        wx = widx_of(g, i)
        seq = g // 4
        pq = (g // 2) % 2
        if i == -1 and g % 2 == 0 and not SKIP_MOE:
            pr = g // 2
            if pr >= 2:
                E.wait("g", ("ptL", pr - 2, NE - 1))
            E.dma("g", lambda gg, pq=pq: gg.dma_start(
                out=T["ptbuf"].ap()[pq * NE * PCAP:(pq + 1) * NE * PCAP, :],
                in_=T["bigc_d"]))
            E.dmark("g", ("ptfill", pr))
        if 1 <= i <= NW:
            pwx = widx_of(g, i - 1)
            opar = pwx % 2
            t0 = (g % 4) * TG + (i - 1) * W
            E.wait("s", ("osum", pwx))
            E.dma("s", lambda ss, opar=opar, t0=t0: ss.dma_start(
                out=out_d[seq, t0:t0 + W, :], in_=otm[opar][:]))
            E.dmark("s", ("outw", pwx))
        if 0 <= i < NW and not SKIP_MOE:
            rpar = wx % 2
            E.wait("g", ("idx", wx))
            E.wait("g", ("xn_tm", wx))
            E.wait("g", ("ptfill", g // 2))
            if g // 2 >= 2:
                E.wait("g", ("xeL", g // 2 - 2, NE - 1, PCT - 1))
            for e in range(NE):
                boff = (pq * NE + e) * PCAP
                E.dma("c", lambda gg, e=e, boff=boff, rpar=rpar: gg.indirect_dma_start(
                    out=T["xbuf"].ap(),
                    out_offset=bass.IndirectOffsetOnAxis(
                        ap=idxi[rpar][:, e:e + 1], axis=0),
                    in_=pl["xntm"][rpar][:], in_offset=None,
                    element_offset=boff * D,
                    bounds_check=REG["cap"], oob_is_err=False))
                E.dma("c", lambda gg, e=e, boff=boff, rpar=rpar: gg.indirect_dma_start(
                    out=T["ptbuf"].ap(),
                    out_offset=bass.IndirectOffsetOnAxis(
                        ap=idxi[rpar][:, e:e + 1], axis=0),
                    in_=T["lpt"][rpar][:, 2 * e:2 * e + 2], in_offset=None,
                    element_offset=boff * 2,
                    bounds_check=REG["cap"], oob_is_err=False))
            E.dbar("c", ("scat", wx))

    def pair_lastwx(p):
        return widx_of(2 * p + 1, NW - 1)

    def moe_loads_x(p, e, trange=None):
        pq = p % 2
        u_e = p * NE + e
        for t in (range(PCT) if trange is None else trange):
            tau = u_e * PCT + t
            if tau >= 4:
                uu = tau - 4
                E.wait(MLQ, ("xeT", uu // PCT // NE, (uu // PCT) % NE, uu % PCT))
            E.wait(MLQ, ("scat", pair_lastwx(p)))
            E.dma(MLQ, lambda gg, e=e, t=t, pq=pq, bi=(u_e * PCT + t) % 4:
                  gg.dma_start(
                      out=xetm[bi][:],
                in_=T["xbuf"].ap()[(pq * NE + e) * PCAP + t * 128:
                                   (pq * NE + e) * PCAP + (t + 1) * 128, :]))
            E.dmark(MLQ, ("xeL", p, e, t))
        if trange is not None and trange.start != 0:
            return
        if u_e >= 2:
            E.wait(MLQ, ("sidx", *divmod(u_e - 2, NE)))
            E.wait(MLQ, ("oev", *divmod(u_e - 2, NE), PCT - 1, 1))
        E.dma(MLQ, lambda gg, e=e, pq=pq, pp=u_e % 2: gg.dma_start(
            out=pslot[pp][:],
            in_=T["ptbuf"].ap()[(pq * NE + e) * PCAP:(pq * NE + e + 1) * PCAP,
                                :].rearrange("(t p) c -> p t c", p=128)))
        E.dmark(MLQ, ("ptL", p, e))

    def _yload(p, e, j):
        u_e = p * NE + e
        u1 = u_e * NJ + j
        slot = u1 % 2
        if u1 >= 2:
            uu = u1 - 2
            E.wait(YQ, ("y1ps", uu // NJ // NE, (uu // NJ) % NE, uu % NJ, 1))
        E.wait(YQ, ("cast_w1", e))
        for h in range(2):
            E.dma(YQ, lambda gg, e=e, j=j, slot=slot, h=h: gg.dma_start(
                out=y1t[:, slot * 1024 + h * 512:slot * 1024 + (h + 1) * 512],
                in_=T["w1bf"].ap()[e, j, :, 4 * h:4 * (h + 1), :].rearrange(
                    "p k c -> p (k c)")))
            if h == 0:
                E.dmark(YQ, ("y1La", p, e, j))
        E.dmark(YQ, ("y1L", p, e, j))
        if u1 >= 2:
            uu = u1 - 2
            E.wait(YQ, ("y2ps", uu // NJ // NE, (uu // NJ) % NE, uu % NJ, 1))
        for h in range(2):
            E.dma(YQ, lambda gg, e=e, j=j, slot=slot, h=h: gg.dma_start(
                out=y2t[:, slot * 1024 + h * 512:slot * 1024 + (h + 1) * 512],
                in_=T["w1bf"].ap()[e, NJ + j, :, 4 * h:4 * (h + 1), :].rearrange(
                    "p k c -> p (k c)")))
            if h == 0:
                E.dmark(YQ, ("y2La", p, e, j))
        E.dmark(YQ, ("y2L", p, e, j))

    def moe_loads_w_head(p, e):
        u_e = p * NE + e
        if u_e >= 1:
            E.wait(MLQ, ("o_ps", *divmod(u_e - 1, NE), PCT - 1, 1))
        E.wait(MLQ, ("cast_w2", e))
        E.dma(MLQ, lambda gg, e=e: gg.dma_start(
            out=w2rt[:], in_=T["w2bf"].ap()[e].rearrange("j p c -> p j c")))
        E.dmark(MLQ, ("w2rL", p, e))

    def moe_y_head(p, e):
        for j in range(2):
            _yload(p, e, j)

    def moe_y_tail(p, e):
        for j in range(2, NJ):
            _yload(p, e, j)

    def moe_expert(p, e):
        pq = p % 2
        seq = p // 2
        lastwx = pair_lastwx(p)
        if True:
            u_e = p * NE + e

            for t in range(PCT):
                E.wait("t", ("xeL", p, e, t))
                bank_acquire("pT", "t")
                for q in range(NQ):
                    E.op("t", lambda tt, q=q, bi=(u_e * PCT + t) % 4:
                         tt.transpose(
                             out=PS["pT"][:, q * 128:(q + 1) * 128],
                             in_=xetm[bi][:, q * 128:(q + 1) * 128],
                             identity=identb[:]))
                E.mark("t", ("xeT", p, e, t))
                E.wait("a", ("xeT", p, e, t))
                E.op("a", lambda a, t=t: a.copy(
                    out=xef[:].rearrange("p (q c) -> p q c", q=NQ)[
                        :, :, t * 128:(t + 1) * 128],
                    in_=PS["pT"][:, 0:1024]))
                E.mark("a", ("xeE", p, e, t))
                bank_release("pT", ("xeE", p, e, t))

            E.wait("v", ("ptL", p, e))
            if u_e >= 1:
                E.wait("v", ("oadd", *divmod(u_e - 1, NE), PCT - 1))
            E.opsync("v", lambda v, pp=u_e % 2: v.tensor_copy(
                out=idxf[:, 0:PCT], in_=pslot[pp][:, 1::2]))
            E.opsync("v", lambda v: v.tensor_copy(out=sidxi[:], in_=idxf[:, 0:PCT]))
            E.mark("v", ("sidx", p, e))

            for j in range(NJ):
                for c in range(2):
                    c0, cw = c * 512, CW[c]
                    uc = u_e * NJ * 2 + j * 2 + c
                    b1 = "pA" if uc % 2 == 0 else "pB"
                    b2 = "pC" if uc % 2 == 0 else "pD"
                    sl = uc % 2
                    slot = (u_e * NJ + j) % 2
                    E.wait("t", ("y1La", p, e, j))
                    E.wait("t", ("y1L", p, e, j))
                    E.wait("t", ("xeE", p, e, PCT - 1))
                    bank_acquire(b1, "t")
                    for k in range(NQ):
                        E.op("t", lambda tt, k=k, b1=b1, slot=slot, c0=c0, cw=cw:
                             tt.matmul(
                                 out=PS[b1][:, 0:cw],
                                 lhsT=y1t[:, (slot * 8 + k) * 128:
                                          (slot * 8 + k + 1) * 128],
                                 rhs=xef[:, k * PCAP + c0:k * PCAP + c0 + cw],
                                 start=(k == 0), stop=(k == NQ - 1)))
                    E.mark("t", ("y1ps", p, e, j, c))
                    E.wait("t", ("y2La", p, e, j))
                    E.wait("t", ("y2L", p, e, j))
                    bank_acquire(b2, "t")
                    for k in range(NQ):
                        E.op("t", lambda tt, k=k, b2=b2, slot=slot, c0=c0, cw=cw:
                             tt.matmul(
                                 out=PS[b2][:, 0:cw],
                                 lhsT=y2t[:, (slot * 8 + k) * 128:
                                          (slot * 8 + k + 1) * 128],
                                 rhs=xef[:, k * PCAP + c0:k * PCAP + c0 + cw],
                                 start=(k == 0), stop=(k == NQ - 1)))
                    E.mark("t", ("y2ps", p, e, j, c))
                    E.wait("a", ("y1ps", p, e, j, c))
                    if uc >= 2:
                        uu = uc - 2
                        E.wait("a", ("g1", uu // 2 // NJ // NE,
                                     (uu // 2 // NJ) % NE, (uu // 2) % NJ,
                                     uu % 2))
                    E.op("a", lambda a, sl=sl, b1=b1, cw=cw: a.activation(
                        out=silu_sb[:, sl * 512:sl * 512 + cw],
                        in_=PS[b1][:, 0:cw], func=AF.Silu))
                    E.mark("a", ("sil", p, e, j, c))
                    bank_release(b1, ("sil", p, e, j, c))
                    E.wait("v", ("sil", p, e, j, c))
                    E.wait("v", ("y2ps", p, e, j, c))
                    if u_e >= 1:
                        E.wait("v", ("o_ps", *divmod(u_e - 1, NE), PCT - 1, 1))
                    E.op("v", lambda v, j=j, sl=sl, b2=b2, c0=c0, cw=cw:
                         v.tensor_tensor(
                             out=gmoe[:, j * PCAP + c0:j * PCAP + c0 + cw],
                             in0=silu_sb[:, sl * 512:sl * 512 + cw],
                             in1=PS[b2][:, 0:cw], op=ALU.mult))
                    E.mark("v", ("g1", p, e, j, c))
                    bank_release(b2, ("g1", p, e, j, c))

            for st in range(PCT):
                for half in range(2):
                    bo = "pE" if (st * 2 + half) % 2 == 0 else "pF"
                    E.wait("t", ("w2rL", p, e))
                    E.wait("t", ("g1", p, e, NJ - 1, 1))
                    bank_acquire(bo, "t")
                    for j in range(NJ):
                        E.op("t", lambda tt, j=j, st=st, half=half, bo=bo: tt.matmul(
                            out=PS[bo][:, 0:512],
                            lhsT=gmoe[:, j * PCAP + st * 128: j * PCAP + st * 128 + 128],
                            rhs=w2rt[:, j * 1024 + half * 512: j * 1024 + (half + 1) * 512],
                            start=(j == 0), stop=(j == NJ - 1)))
                    E.mark("t", ("o_ps", p, e, st, half))
                    E.wait("a", ("o_ps", p, e, st, half))
                    E.wait("a", ("ptL", p, e))
                    if u_e * PCT + st >= 2:
                        uu = u_e * PCT + st - 2
                        E.wait("a", ("oadd", uu // PCT // NE, (uu // PCT) % NE, uu % PCT))
                    E.op("a", lambda a, st=st, half=half, bo=bo, pp=u_e % 2:
                         a.activation(
                             out=o_tok[st % 2][:, half * 512:(half + 1) * 512],
                             in_=PS[bo][:, 0:512], func=AF.Identity,
                             scale=pslot[pp][:, 2 * st:2 * st + 1]))
                    E.mark("a", ("oev", p, e, st, half))
                    bank_release(bo, ("oev", p, e, st, half))
                E.wait("g", ("oev", p, e, st, 1))
                E.wait("g", ("sidx", p, e))
                E.wait("g", ("outw", lastwx))
                if st == 0 and u_e >= 1:
                    for st_ in range(PCT):
                        E.wait("g", ("oadd", *divmod(u_e - 1, NE), st_))
                E.dma("o", lambda gg, st=st, seq=seq: gg.indirect_dma_start(
                    out=out_d.rearrange("b s d -> (b s) d"),
                    out_offset=bass.IndirectOffsetOnAxis(
                        ap=sidxi[:, st:st + 1], axis=0),
                    in_=o_tok[st % 2][:], in_offset=None,
                    element_offset=seq * S * D,
                    bounds_check=REG["tok"], oob_is_err=False,
                    compute_op=ALU.add))
                E.dmark("o", ("oadd", p, e, st))

    NPAIR = NG_RUN // 2
    for g in range(NG_RUN):
        if g == 0:
            sync_loadx(0, 0)
        for i in range(-1, NW + 1):
            pr = g // 2
            il = not SKIP_MOE and pr >= 1 and pr - 1 < IL_UPTO
            s20 = (g % 2) * 10 + (i + 1)
            nwx2 = widx_of(g, i) + 2
            if (0 <= i < NW or (g == 0 and i == -1)) and nwx2 < NG_RUN * NW:
                sync_loadx(nwx2 // NW, nwx2 % NW)
            ev = il and s20 % 2 == 0 and 2 <= s20 <= 18
            m = s20 // 2 - 1 if ev else None
            if ev and m <= NE - 1:
                moe_loads_x(pr - 1, m, range(0, 4))
            sync_slotA(g, i)
            if ev and m <= NE - 1:
                moe_loads_w_head(pr - 1, m)
            sync_woL(g, i)
            if il and s20 % 2 == 1 and 3 <= s20 <= 17:
                moe_loads_x(pr - 1, (s20 - 3) // 2, range(4, PCT))
            pe_slotA(g, i)
            act_slotA(g, i)
            dve_slotA(g, i)
            g_slotA(g, i)
            if ev:
                if m >= 1:
                    moe_y_tail(pr - 1, m - 1)
                if m <= NE - 1:
                    moe_y_head(pr - 1, m)
            if ev and m >= 1:
                moe_expert(pr - 1, m - 1)
            act_nrm(g, i)
    if not SKIP_MOE:
        for p in range(min(IL_UPTO, NPAIR - 1), NPAIR):
            moe_loads_x(p, 0)
            moe_loads_w_head(p, 0)
            moe_y_head(p, 0)
            for e in range(NE):
                moe_y_tail(p, e)
                moe_expert(p, e)
                if e + 1 < NE:
                    moe_loads_x(p, e + 1)
                    moe_loads_w_head(p, e + 1)
                    moe_y_head(p, e + 1)

    if DEBUG:
        E.wait("s", ("outw", widx_of(NG_RUN - 1, NW - 1)))
        for nm in ("cvec", "lnt", "carry", "xng"):
            E.dma("s", lambda s, nm=nm: s.dma_start(
                out=T["dbg_" + nm], in_=T[nm][:]))
        E.dma("s", lambda s: s.dma_start(out=T["dbg_ta"], in_=T["pl"]["ta"][:]))
        for nm in ("tx", "rho"):
            for p_ in range(2):
                E.dma("s", lambda s, nm=nm, p_=p_: s.dma_start(
                    out=T[f"dbg_{nm}{p_}"], in_=T["pl"][nm][p_][:]))
        for nm in ("hm", "cs", "sn", "p1", "p2", "hr", "hi", "pv", "pnrm0",
                   "pnrm1", "xh0", "xh1"):
            if nm in ("pnrm0", "pnrm1"):
                src = T["pl"]["pnrm"][int(nm[-1])]
            elif nm in ("xh0", "xh1"):
                src = T["pl"]["xh"][int(nm[-1])]
            else:
                src = T["pl"][nm]
            E.dma("s", lambda s, nm=nm, src=src: s.dma_start(
                out=T["dbg_" + nm], in_=src[:]))
        E.drain_dma("s", "s")

    E.wait("s", ("outw", widx_of(NG_RUN - 1, NW - 1)))
    E.wait("v", ("outw", widx_of(NG_RUN - 1, NW - 1)))
    E.drain_dma("s", "g")
    E.drain_dma("s", "s")
    E.drain_dma("s", "o")
    E.drain_dma("v", "g")
    E.drain_dma("v", "o")



def _build_nc():
    nc = bass.Bass("TRN2", target_bir_lowering=False)
    T = {}
    T["x"] = nc.dram_tensor("x", [2, S, D], F32, kind="ExternalInput").ap()
    T["out"] = nc.dram_tensor("out", [2, S, D], F32, kind="ExternalOutput").ap()
    T["w_in_d"] = nc.dram_tensor("w_in_t", [NMH, 128, NQ, 128], F32, kind="ExternalInput").ap()
    T["w_out_d"] = nc.dram_tensor("w_out_t", [NQ, 128, NMH, 128], F32, kind="ExternalInput").ap()
    T["w_gx_d"] = nc.dram_tensor("w_gx_t", [128, 8, 2, 128], F32, kind="ExternalInput").ap()
    T["w_ga_d"] = nc.dram_tensor("w_ga_t", [128, 8, 2, 128], F32, kind="ExternalInput").ap()
    T["w1_d"] = nc.dram_tensor("w1_t", [NE, 22, 128, NQ, 128], F32, kind="ExternalInput").ap()
    T["w2_d"] = nc.dram_tensor("w2_t", [NE, NJ, 128, 1024], F32, kind="ExternalInput").ap()
    T["rw_d"] = nc.dram_tensor("rw_t", [128, NQ, 8], F32, kind="ExternalInput").ap()
    T["cin_d"] = nc.dram_tensor("cin", [128, 32], F32, kind="ExternalInput").ap()
    T["binv_d"] = nc.dram_tensor("binv", [128, 16], F32, kind="ExternalInput").ap()
    T["boutv_d"] = nc.dram_tensor("boutv", [128, 8], F32, kind="ExternalInput").ap()
    T["identf_d"] = nc.dram_tensor("identf", [128, 128], F32, kind="ExternalInput").ap()
    T["identb_d"] = nc.dram_tensor("identb", [128, 128], BF16, kind="ExternalInput").ap()
    T["iota_d"] = nc.dram_tensor("iota", [128, 1], F32, kind="ExternalInput").ap()
    T["bigc_d"] = nc.dram_tensor("bigc", [NE, PCAP, 2], F32, kind="ExternalInput").ap()
    if DEBUG:
        dbg_specs = [("cvec", [128, 64], F32), ("lnt", [128, 12], F32),
                     ("carry", [128, 24], F32), ("ta", [128, 1024], F32),
                     ("xng", [128, NQ * TG], BF16),
                     ("tx0", [128, 1024], F32), ("tx1", [128, 1024], F32),
                     ("rho0", [128, 1024], F32), ("rho1", [128, 1024], F32),
                     ("hm", [128, 2048], BF16), ("cs", [128, 1024], F32),
                     ("sn", [128, 1024], F32), ("p1", [128, 1024], F32),
                     ("p2", [128, 1024], F32), ("hr", [128, 1024], F32),
                     ("hi", [128, 1024], F32), ("pv", [128, 1024], F32),
                     ("pnrm0", [128, 1024], F32), ("pnrm1", [128, 1024], F32),
                     ("xh0", [128, 2048], BF16), ("xh1", [128, 2048], BF16)]
        for nm, shp, dt in dbg_specs:
            T["dbg_" + nm] = nc.dram_tensor("dbg_" + nm, shp, dt,
                                            kind="ExternalOutput").ap()
    T["wibf"] = nc.dram_tensor("wibf", [NMH, 128, NQ, 128], BF16, kind="Internal")
    T["wobf"] = nc.dram_tensor("wobf", [NQ, 128, NMH, 128], BF16, kind="Internal")
    T["w1bf"] = nc.dram_tensor("w1bf", [NE, 22, 128, NQ, 128], BF16, kind="Internal")
    T["w2bf"] = nc.dram_tensor("w2bf", [NE, NJ, 128, 1024], BF16, kind="Internal")
    T["xbuf"] = nc.dram_tensor("xbuf", [2 * NE * PCAP, D], BF16, kind="Internal")
    T["ptbuf"] = nc.dram_tensor("ptbuf", [2 * NE * PCAP, 2], F32, kind="Internal")

    with contextlib.ExitStack() as ctx:
        def sb(name, shape, dt):
            return ctx.enter_context(nc.sbuf_tensor(name, shape, dt))

        pl = {}
        pl["xtm"] = [sb("xtm0", [128, 1024], F32), sb("xtm1", [128, 1024], F32),
                     sb("xtm2", [128, 1024], F32)]
        pl["xntm"] = [sb("xntm0", [128, 1024], BF16), sb("xntm1", [128, 1024], BF16)]
        pl["xh"] = [sb("xh0", [128, 2048], BF16), sb("xh1", [128, 2048], BF16),
                    sb("xh2", [128, 2048], BF16)]
        pl["tx"] = [sb("tx0", [128, 1024], F32), sb("tx1", [128, 1024], F32)]
        pl["ta"] = sb("ta", [128, 1024], F32)
        pl["rho"] = [sb("rho0", [128, 1024], F32), sb("rho1", [128, 1024], F32)]
        pl["pnrm"] = [sb("pnrm0", [128, 1024], F32), sb("pnrm1", [128, 1024], F32)]
        pl["hm"] = sb("hm", [128, 2048], BF16)
        for nm in ("pv", "p1", "p2", "cs", "sn", "hr", "hi"):
            pl[nm] = sb(nm, [128, 1024], F32)
        pl["psq"] = pl["hr"]
        pl["pi"] = sb("pi", [128, 512], I32)
        T["pl"] = pl

        T["cvec"] = sb("cvec", [128, 64], F32)
        T["cin"] = sb("cins", [128, 32], F32)
        T["binv"] = sb("binvs", [128, 16], F32)
        T["lnt"] = sb("lnt", [128, 12], F32)
        T["carry"] = sb("carry", [128, 24], F32)
        T["xng"] = sb("xng", [128, NQ * 2 * W], BF16)
        T["gmoe"] = sb("gmoe", [128, NJ * PCAP], BF16)
        T["wgates"] = sb("wgates", [128, 4096], BF16)
        T["rw_sb"] = sb("rw_sb", [128, 64], BF16)
        T["win_sb"] = sb("win_sb", [128, 3072], BF16)
        T["wout_sb"] = sb("wout_sb", [128, 4096], BF16)
        T["y1t"] = sb("y1t", [128, 2048], BF16)
        T["y2t"] = sb("y2t", [128, 2048], BF16)
        T["w2rt"] = sb("w2rt", [128, NJ * 1024], BF16)
        T["silu_sb"] = sb("silu_sb", [128, 768], BF16)
        T["identf"] = sb("identfs", [128, 128], F32)
        T["identb"] = sb("identbs", [128, 128], BF16)
        T["iota_sb"] = sb("iota_sb", [128, 1], F32)
        T["rlg"] = sb("rlg", [8, 128], F32)
        T["ltok"] = sb("ltokt", [128, 8], F32)
        T["lp"] = [sb("lp0", [128, 8], F32), sb("lp1", [128, 8], F32)]
        T["msk"] = sb("msk", [128, 8], F32)
        T["rtm"] = sb("rtm", [128, 4], F32)
        T["mfm"] = sb("mfm", [8, 128], F32)
        T["posf"] = sb("posf", [8, 128], F32)
        T["rcount"] = sb("rcount", [8, 1], F32)
        T["idxf"] = sb("idxf", [128, 8], F32)
        T["idxi"] = [sb("idxi0", [128, 8], I32), sb("idxi1", [128, 8], I32)]
        T["tokv"] = [sb("tokv0", [128, 1], F32), sb("tokv1", [128, 1], F32)]
        T["lpt"] = [sb("lpt0", [128, 16], F32), sb("lpt1", [128, 16], F32)]
        T["xef"] = sb("xef", [128, NQ * PCAP], BF16)
        T["xetm"] = [sb(f"xetm{i}", [128, 1024], BF16) for i in range(4)]
        T["o_tok"] = [sb("o_tok0", [128, 1024], F32), sb("o_tok1", [128, 1024], F32)]
        T["pslot"] = [sb("pslot0", [128, 2 * PCT], F32),
                      sb("pslot1", [128, 2 * PCT], F32)]
        T["sidxi"] = sb("sidxi", [128, PCT], I32)
        T["yfm"] = sb("yfm", [128, 1024], F32)
        T["otm"] = [sb("otm0", [128, 1024], F32), sb("otm1", [128, 1024], F32)]

        PS = {}
        for nm in ("pA", "pB", "pC", "pD", "pE", "pF", "pR"):
            PS[nm] = ctx.enter_context(nc.psum_tensor(nm, [128, 512], F32))
        PS["pT"] = ctx.enter_context(nc.psum_tensor("pT", [128, 1024], BF16))
        T["ps"] = PS

        sems = {}
        for stname in ("t", "v", "a"):
            sems[stname] = ctx.enter_context(nc.semaphore("sem_" + stname))
        for stname, nl in DMA_LANES.items():
            for lane in range(nl):
                key = stname + str(lane)
                sems[key] = ctx.enter_context(nc.semaphore("sem_" + key))
        block = ctx.enter_context(nc.Block())

        E1 = Em(1)
        _emit(E1, T)
        marks = E1.marks

        def section(stname):
            def run(engine):
                E2 = Em(2, marks)
                E2.active = stname
                E2.engines = {stname: engine}
                E2.sems = sems
                _emit(E2, T)
            return run

        block.sync(section("s"))
        block.gpsimd(section("g"))
        block.tensor(section("t"))
        block.scalar(section("a"))
        block.vector(section("v"))

    return nc


def _elide_ldweights(nc):
    n_elided = 0
    for f in nc.m.functions:
        for bb in f.blocks:
            insl = bb.instructions
            prev_key = None
            for k, ins in enumerate(insl):
                tn = type(ins).__name__
                if tn == "InstMatmult":
                    if ins.is_transpose or ins.ins[1].dtype != BF16 \
                            or ins.ins[0].dtype != BF16:
                        prev_key = None
                        continue
                    key = (str(ins.ins[1]), str(ins.tile_position),
                           str(ins.perf_mode))
                    if prev_key is not None and key == prev_key:
                        insl[k] = dataclasses.replace(ins, ldweights=False)
                        n_elided += 1
                    prev_key = key
                elif tn in ("InstEventSemaphore", "InstNop"):
                    continue
                else:
                    prev_key = None
    return n_elided


_NC_CACHE = None


def kernel(**inputs):
    global _NC_CACHE
    x, com = _prep(inputs)
    if _NC_CACHE is None:
        _NC_CACHE = _build_nc()
        _elide_ldweights(_NC_CACHE)
    nc = _NC_CACHE
    in_maps = []
    for c in range(NCORES):
        m = dict(com)
        m["x"] = np.ascontiguousarray(x[2 * c:2 * c + 2])
        in_maps.append(m)
    res = run_bass_kernel_spmd(nc, in_maps, core_ids=list(range(NCORES)))
    out = np.empty((16, S, D), dtype=np.float32)
    for c in range(NCORES):
        out[2 * c:2 * c + 2] = np.asarray(res.results[c]["out"])
    return out
